# version 17
# baseline (speedup 1.0000x reference)
import numpy as np
import ml_dtypes
from contextlib import ExitStack

import concourse.bass as bass
import concourse.mybir as mybir
from concourse.bass_utils import run_bass_kernel_spmd

F32 = mybir.dt.float32
BF16 = mybir.dt.bfloat16
AF = mybir.ActivationFunctionType
ALU = mybir.AluOpType

B, S, D = 4, 4096, 1024
H, DH = 16, 64
FF, R = 4096, 256
TOK = 2048
NT = 16
NG = 4
EPS = 1e-5
N_CORES = 8


class _Op:
    __slots__ = ("eng", "emit", "deps", "marked", "kind", "tag", "handle", "pos",
                 "dsem")

    def __init__(self, eng, emit, kind, tag):
        self.eng, self.emit, self.kind, self.tag = eng, emit, kind, tag
        self.deps = []
        self.marked = False
        self.handle = None
        self.pos = 0
        self.dsem = None


class Sched:

    def __init__(self):
        self.streams = {e: [] for e in ("pe", "act", "dve", "pool", "sp")}
        self.last_w = {}
        self.readers = {}
        self.dgroups = {}

    def add(self, eng, emit, reads=(), writes=(), kind="c", tag="", dsem=None,
            dmode="batch"):
        op = _Op(eng, emit, kind, tag)
        if kind == "d":
            if dsem is None:
                dsem, dmode = f"_d_{tag or len(self.dgroups)}", "batch"
            op.dsem = dsem
            self.dgroups.setdefault(dsem, [dmode, []])[1].append(op)
            self.dgroups[dsem][0] = dmode
        op.pos = len(self.streams[eng])
        deps = set()
        for k in reads:
            w = self.last_w.get(k)
            if w is not None:
                deps.add(w)
        for k in writes:
            w = self.last_w.get(k)
            if w is not None:
                deps.add(w)
            for r in self.readers.get(k, ()):
                deps.add(r)
        for d in deps:
            if d is op:
                continue
            if (d.eng == eng and d.kind == "c" and kind == "c" and eng == "pe"):
                continue
            op.deps.append(d)
        self.streams[eng].append(op)
        for k in writes:
            self.last_w[k] = op
            self.readers[k] = []
        for k in reads:
            self.readers.setdefault(k, []).append(op)
        return op

    def finalize(self, nc, es):
        sems = {}
        for e in ("pe", "act", "dve", "pool"):
            sems[e] = es.enter_context(nc.semaphore(f"sem_{e}"))
        sems["cc"] = es.enter_context(nc.semaphore("sem_cc"))
        for gname in self.dgroups:
            sems[gname] = es.enter_context(nc.semaphore(f"sem_{gname}"))

        for ops in self.streams.values():
            for op in ops:
                for d in op.deps:
                    d.marked = True
        self._dfinal = {}
        for gname, (mode, gops) in self.dgroups.items():
            total = 16 * len(gops)
            for i, op in enumerate(gops):
                op.handle = (gname, 16 * (i + 1) if mode == "serial" else total)
                op.marked = True
            self._dfinal[gname] = total
        for eng, ops in self.streams.items():
            nc_, ncc = 0, 0
            for op in ops:
                if op.kind == "cc":
                    ncc += 1
                    op.handle = ("cc", ncc)
                    op.marked = True
                elif op.kind == "c" and op.marked:
                    nc_ += 1
                    op.handle = (eng, nc_)

        block = es.enter_context(nc.Block())
        engmap = {"pe": block.tensor, "act": block.scalar, "dve": block.vector,
                  "pool": block.gpsimd, "sp": block.sync}
        sched = self

        def emit_stream(eng_name):
            def body(eng):
                waited = {}
                for op in sched.streams[eng_name]:
                    for d in sorted(op.deps, key=lambda o: (o.handle[0], o.handle[1])):
                        sem_name, val = d.handle
                        if waited.get(sem_name, 0) < val:
                            eng.wait_ge(sems[sem_name], val)
                            waited[sem_name] = val
                    inst = op.emit(eng)
                    if op.marked:
                        sem_name, val = op.handle
                        inst.then_inc(sems[sem_name], 16 if op.kind == "d" else 1)
                if eng_name == "sp":
                    for gname, total in sorted(sched._dfinal.items()):
                        if waited.get(gname, 0) < total:
                            eng.wait_ge(sems[gname], total)
            return body

        for name, deco in engmap.items():
            deco(emit_stream(name))


def build_program():
    import os as _os
    PH = int(_os.environ.get("K_PHASES", "5"))
    nc = bass.Bass("TRN2", target_bir_lowering=False)

    x_h = nc.dram_tensor("x", [TOK, D], F32, kind="ExternalInput").ap()
    wqk_h = nc.dram_tensor("Wqk", [D, 2 * D], F32, kind="ExternalInput").ap()
    wv_h = nc.dram_tensor("Wv", [D, D], F32, kind="ExternalInput").ap()
    wn_h = nc.dram_tensor("Wn", [D, H], F32, kind="ExternalInput").ap()
    wo_h = nc.dram_tensor("Wo", [D, D], F32, kind="ExternalInput").ap()
    u1_h = nc.dram_tensor("U1", [D, R], F32, kind="ExternalInput").ap()
    v1_h = nc.dram_tensor("V1", [R, FF], F32, kind="ExternalInput").ap()
    u2_h = nc.dram_tensor("U2", [FF, R], F32, kind="ExternalInput").ap()
    v2_h = nc.dram_tensor("V2", [R, D], F32, kind="ExternalInput").ap()
    bqk_h = nc.dram_tensor("bqk_t", [128, 16], F32, kind="ExternalInput").ap()
    bn_h = nc.dram_tensor("bn_t", [16, 1], F32, kind="ExternalInput").ap()
    b1_h = nc.dram_tensor("b1_t", [128, 32], F32, kind="ExternalInput").ap()
    bv_h = nc.dram_tensor("bv_r", [1, D], F32, kind="ExternalInput").ap()
    bo_h = nc.dram_tensor("bo_r", [1, D], F32, kind="ExternalInput").ap()
    b2_h = nc.dram_tensor("b2_r", [1, D], F32, kind="ExternalInput").ap()
    mask_h = nc.dram_tensor("mask4", [128, 512], BF16, kind="ExternalInput").ap()
    ident_h = nc.dram_tensor("ident", [128, 128], BF16, kind="ExternalInput").ap()
    sel_h = nc.dram_tensor("sel", [16, 1024], BF16, kind="ExternalInput").ap()
    ones_h = nc.dram_tensor("ones_r", [1, 128], BF16, kind="ExternalInput").ap()
    uv_h = nc.dram_tensor("uv", [128, 2], F32, kind="ExternalInput").ap()
    out_h = nc.dram_tensor("out", [TOK, D], F32, kind="ExternalOutput").ap()

    cc_in = nc.dram_tensor("cc_in", [128, 512], F32, kind="Internal").ap()
    cc_out = nc.dram_tensor("cc_out", [128, 512], F32, kind="Internal").ap()

    es = ExitStack()
    with es:
        sA = es.enter_context(nc.sbuf_tensor("sA", [128, 16384], BF16))
        sB = es.enter_context(nc.sbuf_tensor("sB", [128, 16384], BF16))
        sC = es.enter_context(nc.sbuf_tensor("sC", [128, 16384], BF16))
        sD = es.enter_context(nc.sbuf_tensor("sD", [128, 16384], BF16))
        sE = es.enter_context(nc.sbuf_tensor("sE", [128, 16384], BF16))
        wVQ = es.enter_context(nc.sbuf_tensor("wVQ", [128, 8192], BF16))
        smb = es.enter_context(nc.sbuf_tensor("smb", [128, 4480], BF16))
        xf32 = es.enter_context(nc.sbuf_tensor("xf32", [128, 3072], F32))
        lnscr = es.enter_context(nc.sbuf_tensor("lnscr", [128, 136], F32))
        fscr = es.enter_context(nc.sbuf_tensor("fscr", [128, 2048], F32))
        srun = es.enter_context(nc.sbuf_tensor("srun", [128, 512], F32))
        bqk_sb = es.enter_context(nc.sbuf_tensor("bqk_sb", [128, 16], F32))
        b1_sb = es.enter_context(nc.sbuf_tensor("b1_sb", [128, 32], F32))
        bn_sb = es.enter_context(nc.sbuf_tensor("bn_sb", [16, 2], F32))
        pb = [es.enter_context(nc.psum_tensor(f"pb{i}", [128, 512], F32))
              for i in range(7)]
        ptA = es.enter_context(nc.psum_tensor("ptA", [128, 1024], BF16))

        def xn(kc, t0, w):
            return sA[:, kc * 2048 + t0:kc * 2048 + t0 + w]
        XNSCR = (sB, 0)
        PSB = (sC, 2048)
        HSCR = (sC, 3072)
        QSB = (sC, 4096)
        WK0 = 0
        QB0 = 0
        KTM0 = 8192
        SC0 = 10240
        MASK = smb[:, 0:512]
        IDENT = smb[:, 512:640]
        EN0 = 640
        SEL0 = 2688
        BVR, BOR, B2R = 640, 1664, 2688
        SRBF0 = 3712
        ONES0 = 4224
        WN0 = 4352
        FACT0, GATH0, SLOC0 = 0, 512, 1024
        SQD = 0

        sch = Sched()

        def dma_pool(emit, writes, reads=(), tag="", dsem=None, dmode="batch"):
            sch.add("pool", emit, reads=reads, writes=writes, kind="d", tag=tag,
                    dsem=dsem, dmode=dmode)

        def dma_sp(emit, writes, reads=(), tag="", dsem=None, dmode="batch"):
            sch.add("sp", emit, reads=reads, writes=writes, kind="d", tag=tag,
                    dsem=dsem, dmode=dmode)

        dma_sp(lambda e: e.dma_start(out=smb[:, 0:512], in_=mask_h), ["mask"], dsem="spsm")
        dma_sp(lambda e: e.dma_start(out=smb[:, 512:640], in_=ident_h), ["ident"], dsem="spsm")
        dma_sp(lambda e: e.dma_start(out=smb[0:16, SEL0:SEL0 + 1024], in_=sel_h), ["sel"], dsem="spsm")
        dma_sp(lambda e: e.dma_start(out=smb[32:33, ONES0:ONES0 + 128], in_=ones_h), ["ones"], dsem="spsm")
        dma_sp(lambda e: e.dma_start(out=bqk_sb[:, :], in_=bqk_h), ["bqk"], dsem="spsm")
        dma_sp(lambda e: e.dma_start(out=b1_sb[:, :], in_=b1_h), ["b1"], dsem="spsm")
        dma_sp(lambda e: e.dma_start(out=bn_sb[:, 0:1], in_=bn_h), ["bn"], dsem="spsm")
        dma_sp(lambda e: e.dma_start(out=lnscr[:, 128:130], in_=uv_h), ["uv"], dsem="spsm")
        dma_pool(lambda e: e.dma_start(
            out=wVQ[:, :].rearrange("p (kc d) -> p kc d", kc=8),
            in_=wv_h.rearrange("(kc p) d -> p kc d", p=128)), ["wWv"], tag="ldWv")
        dma_pool(lambda e: e.dma_start(
            out=sC[:, 0:2048].rearrange("p (kc r) -> p kc r", kc=8),
            in_=u1_h.rearrange("(kc p) r -> p kc r", p=128)), ["wU1"], tag="ldU1")
        dma_pool(lambda e: e.dma_start(
            out=sE[:, 0:8192].rearrange("p (rc f) -> p rc f", rc=2),
            in_=v1_h.rearrange("(rc p) f -> p rc f", p=128)), ["wV1"], tag="ldV1")
        dma_pool(lambda e: e.dma_start(
            out=sE[:, 8192:16384].rearrange("p (fc r) -> p fc r", fc=32),
            in_=u2_h.rearrange("(fc p) r -> p fc r", p=128)), ["wU2"], tag="ldU2")
        dma_pool(lambda e: e.dma_start(
            out=sC[:, 5120:7168].rearrange("p (rc d) -> p rc d", rc=2),
            in_=v2_h.rearrange("(rc p) d -> p rc d", p=128)), ["wV2"], tag="ldV2")
        dma_pool(lambda e: e.dma_start(out=smb[32:33, B2R:B2R + 1024], in_=b2_h), ["b2r"], dsem="pw")
        dma_pool(lambda e: e.dma_start(out=smb[32:33, BVR:BVR + 1024], in_=bv_h), ["bvr"], dsem="pw")
        dma_pool(lambda e: e.dma_start(out=smb[32:33, BOR:BOR + 1024], in_=bo_h), ["bor"], dsem="pw")
        dma_pool(lambda e: e.dma_start(
            out=smb[:, WN0:WN0 + 128].rearrange("p (kc h) -> p kc h", kc=8),
            in_=wn_h.rearrange("(kc p) h -> p kc h", p=128)),
            ["wWn"], tag="ldWn", dsem="pw")
        sch.add("dve", lambda e: e.tensor_scalar(bn_sb[0:16, 1:2], bn_sb[0:16, 0:1],
                                                 -1.0, None, ALU.mult),
                reads=["bn"], writes=["bnneg"], tag="bnneg")
        sch.add("dve", lambda e: e.memset(lnscr[:, 130:131], EPS),
                reads=(), writes=["epsc"], tag="epsset")

        def emit_A_ln(t):
            buf = t % 3

            def em(e, t=t, buf=buf):
                return e.dma_start(out=xf32[:, buf * 1024:(buf + 1) * 1024],
                                   in_=x_h[t * 128:(t + 1) * 128, :])
            dma_sp(em, [f"xf.{buf}"], tag=f"ldx{t}", dsem=f"xf{buf}", dmode="serial")

            c0 = t * 8

            def r_sum(e, t=t, buf=buf, c0=c0):
                return e.tensor_reduce(lnscr[:, c0:c0 + 1],
                                       xf32[:, buf * 1024:(buf + 1) * 1024],
                                       mybir.AxisListType.X, ALU.add, negate=True)
            sch.add("dve", r_sum, reads=[f"xf.{buf}"], writes=[f"ln.{t}"], tag=f"lnsum{t}")

            def r_sq(e, t=t, buf=buf, c0=c0):
                return e.activation(fscr[:, SQD:SQD + 1024],
                                    xf32[:, buf * 1024:(buf + 1) * 1024],
                                    AF.Square, accum_out=lnscr[:, c0 + 1:c0 + 2])
            sch.add("act", r_sq, reads=[f"xf.{buf}"], writes=["dsq", f"lnsq.{t}"], tag=f"lnsq{t}")

            def r_m(e, c0=c0):
                return e.tensor_scalar(lnscr[:, c0 + 2:c0 + 3], lnscr[:, c0:c0 + 1],
                                       1.0 / D, None, ALU.mult)
            sch.add("dve", r_m, reads=[f"ln.{t}"], writes=[f"lnm.{t}"], tag=f"lnm{t}")

            def r_msq(e, c0=c0):
                return e.tensor_tensor(lnscr[:, c0 + 3:c0 + 4], lnscr[:, c0 + 2:c0 + 3],
                                       lnscr[:, c0 + 2:c0 + 3], ALU.mult)
            sch.add("dve", r_msq, reads=[f"lnm.{t}"], writes=[f"lnmsq.{t}"], tag=f"lnmsq{t}")

            def r_var(e, c0=c0):
                return e.scalar_tensor_tensor(lnscr[:, c0 + 4:c0 + 5],
                                              lnscr[:, c0 + 1:c0 + 2], 1.0 / D,
                                              lnscr[:, c0 + 3:c0 + 4],
                                              ALU.mult, ALU.subtract)
            sch.add("dve", r_var, reads=[f"lnsq.{t}", f"lnmsq.{t}"],
                    writes=[f"lnvar.{t}"], tag=f"lnvar{t}")

            def r_std(e, c0=c0):
                return e.activation(lnscr[:, c0 + 5:c0 + 6], lnscr[:, c0 + 4:c0 + 5],
                                    AF.Sqrt, bias=lnscr[:, 130:131])
            sch.add("act", r_std, reads=[f"lnvar.{t}", "epsc"], writes=[f"lnstd.{t}"],
                    tag=f"lnstd{t}")

            def r_rstd(e, c0=c0):
                return e.reciprocal(lnscr[:, c0 + 6:c0 + 7], lnscr[:, c0 + 5:c0 + 6])
            sch.add("dve", r_rstd, reads=[f"lnstd.{t}"], writes=[f"lnrstd.{t}"], tag=f"lnrstd{t}")

            sbuf_t, scr0 = XNSCR
            sbuf2 = t % 2
            xcol = scr0 + sbuf2 * 1024

            def r_xn(e, t=t, buf=buf, c0=c0, xcol=xcol):
                return e.tensor_scalar(sB[:, xcol:xcol + 1024],
                                       xf32[:, buf * 1024:(buf + 1) * 1024],
                                       lnscr[:, c0 + 2:c0 + 3],
                                       lnscr[:, c0 + 6:c0 + 7],
                                       ALU.add, ALU.mult)
            sch.add("dve", r_xn, reads=[f"xf.{buf}", f"lnm.{t}", f"lnrstd.{t}"],
                    writes=[f"xnscr.{sbuf2}"], tag=f"xn{t}")

        def emit_A_tr(t):
            sbuf2 = t % 2
            xcol = XNSCR[1] + sbuf2 * 1024
            for kc in range(8):
                def r_tr(e, kc=kc, xcol=xcol):
                    return e.transpose(ptA[:, kc * 128:(kc + 1) * 128],
                                       sB[:, xcol + kc * 128:xcol + (kc + 1) * 128],
                                       smb[:, 512:640])
                sch.add("pe", r_tr, reads=[f"xnscr.{sbuf2}", "ident"],
                        writes=["ptA"], tag=f"trA{t}.{kc}")

            def r_ev(e, t=t):
                return e.tensor_copy(
                    sA[:, :].rearrange("p (kc c) -> p kc c", kc=8)[:, :, t * 128:(t + 1) * 128],
                    ptA[:, :].rearrange("p (kc n) -> p kc n", kc=8))
            sch.add("dve", r_ev, reads=["ptA"], writes=[f"xn.{t}"], tag=f"evA{t}")

        def emit_A_tile(t):
            emit_A_ln(t)
            emit_A_tr(t)

        def emit_C_group(g, a_tiles=()):
            t0 = g * 512
            xn_keys = [f"xn.{4 * g + i}" for i in range(4)]
            for rg in range(2):
                bank = pb[rg]
                for kc in range(8):
                    def r_mm(e, rg=rg, kc=kc, t0=t0, bank=bank):
                        return e.matmul(bank[:, :],
                                        sC[:, kc * 256 + rg * 128:kc * 256 + (rg + 1) * 128],
                                        xn(kc, t0, 512),
                                        start=(kc == 0), stop=(kc == 7))
                    sch.add("pe", r_mm, reads=["wU1"] + xn_keys, writes=[f"pb{rg}"],
                            tag=f"P{g}.{rg}.{kc}")

                def r_ev(e, rg=rg, bank=bank):
                    return e.copy(sC[:, PSB[1] + rg * 512:PSB[1] + (rg + 1) * 512], bank[:, :])
                sch.add("act", r_ev, reads=[f"pb{rg}"], writes=[f"P.{rg}"], tag=f"Pev{g}.{rg}")

            def emit_H(fc):
                hb = pb[fc % 4]
                for rc in range(2):
                    def r_mm(e, fc=fc, rc=rc, hb=hb):
                        return e.matmul(hb[:, :],
                                        sE[:, rc * 4096 + fc * 128:rc * 4096 + (fc + 1) * 128],
                                        sC[:, PSB[1] + rc * 512:PSB[1] + (rc + 1) * 512],
                                        start=(rc == 0), stop=(rc == 1))
                    sch.add("pe", r_mm, reads=["wV1", "P.0", "P.1"],
                            writes=[f"pb{fc % 4}"], tag=f"H{g}.{fc}.{rc}")

                hcol = (3072, 3584, 7168, 7680)[fc % 4]

                def r_gelu(e, fc=fc, hb=hb, hcol=hcol):
                    return e.activation(sC[:, hcol:hcol + 512], hb[:, :],
                                        AF.Gelu_apprx_tanh, bias=b1_sb[:, fc:fc + 1])
                sch.add("act", r_gelu, reads=[f"pb{fc % 4}", "b1"],
                        writes=[f"Hs.{fc % 4}"], tag=f"gelu{g}.{fc}")

            def emit_Q(fc):
                hcol = (3072, 3584, 7168, 7680)[fc % 4]
                for rg in range(2):
                    qb = pb[4 + rg]

                    def r_mm(e, fc=fc, rg=rg, qb=qb, hcol=hcol):
                        return e.matmul(qb[:, :],
                                        sE[:, 8192 + fc * 256 + rg * 128:8192 + fc * 256 + (rg + 1) * 128],
                                        sC[:, hcol:hcol + 512],
                                        start=(fc == 0), stop=(fc == 31))
                    sch.add("pe", r_mm, reads=["wU2", f"Hs.{fc % 4}"],
                            writes=[f"pb{4 + rg}"], tag=f"Q{g}.{fc}.{rg}")

            emit_H(0)
            emit_H(1)
            for fc in range(2, 32, 2):
                emit_H(fc)
                emit_H(fc + 1)
                emit_Q(fc - 2)
                emit_Q(fc - 1)
            emit_Q(30)
            emit_Q(31)

            for rg in range(2):
                def r_ev(e, rg=rg):
                    return e.copy(sC[:, QSB[1] + rg * 512:QSB[1] + (rg + 1) * 512],
                                  pb[4 + rg][:, :])
                sch.add("act", r_ev, reads=[f"pb{4 + rg}"], writes=[f"Qs.{rg}"], tag=f"Qev{g}.{rg}")

            for ti in range(4):
                t = 4 * g + ti
                for hf in range(2):
                    fb = pb[hf]
                    for rc in range(2):
                        def r_mm(e, ti=ti, hf=hf, rc=rc, fb=fb):
                            return e.matmul(fb[:, :],
                                            sC[:, QSB[1] + rc * 512 + ti * 128:QSB[1] + rc * 512 + (ti + 1) * 128],
                                            sC[:, 5120 + rc * 1024 + hf * 512:5120 + rc * 1024 + (hf + 1) * 512],
                                            start=(rc == 0), stop=False)
                        sch.add("pe", r_mm, reads=["wV2", "Qs.0", "Qs.1"],
                                writes=[f"pb{hf}"], tag=f"ffn{t}.{hf}.{rc}")

                    def r_bias(e, hf=hf, fb=fb):
                        return e.matmul(fb[:, :],
                                        smb[32:33, ONES0:ONES0 + 128],
                                        smb[32:33, B2R + hf * 512:B2R + (hf + 1) * 512],
                                        start=False, stop=True)
                    sch.add("pe", r_bias, reads=["ones", "b2r"], writes=[f"pb{hf}"],
                            tag=f"ffnb{t}.{hf}")

                    def r_silu(e, t=t, hf=hf, fb=fb):
                        return e.activation(sB[:, t * 1024 + hf * 512:t * 1024 + (hf + 1) * 512],
                                            fb[:, :], AF.Silu)
                    gw = [f"g.{t}.{hf}"]
                    if t < 2:
                        gw += ["xnscr.0", "xnscr.1"]
                    sch.add("act", r_silu, reads=[f"pb{hf}"], writes=gw,
                            tag=f"silu{t}.{hf}")

        def emit_Vproj(t):
            for hf in range(2):
                bank = pb[2 + (2 * t + hf) % 4]
                for kc in range(8):
                    def r_mm(e, t=t, hf=hf, kc=kc, bank=bank):
                        return e.matmul(bank[:, :],
                                        xn(kc, t * 128, 128),
                                        wVQ[:, kc * 1024 + hf * 512:kc * 1024 + (hf + 1) * 512],
                                        start=(kc == 0), stop=False)
                    sch.add("pe", r_mm, reads=["wWv", f"xn.{t}"],
                            writes=[f"pb{2 + (2 * t + hf) % 4}"], tag=f"v{t}.{hf}.{kc}")

                def r_bias(e, hf=hf, bank=bank):
                    return e.matmul(bank[:, :],
                                    smb[32:33, ONES0:ONES0 + 128],
                                    smb[32:33, BVR + hf * 512:BVR + (hf + 1) * 512],
                                    start=False, stop=True)
                sch.add("pe", r_bias, reads=["ones", "bvr"], writes=[f"pb{2 + (2 * t + hf) % 4}"],
                        tag=f"vb{t}.{hf}")

                def r_ev(e, t=t, hf=hf, bank=bank):
                    return e.tensor_copy(sD[:, t * 1024 + hf * 512:t * 1024 + (hf + 1) * 512],
                                         bank[:, :])
                sch.add("dve", r_ev, reads=[f"pb{2 + (2 * t + hf) % 4}"],
                        writes=[f"vtm.{t}.{hf}"], tag=f"vev{t}.{hf}")

        emit_A_ln(0)
        for t in range(NT):
            emit_A_tr(t)
            if t + 1 < NT:
                emit_A_ln(t + 1)
            emit_Vproj(t)
        for g in range(NG):
            if PH >= 2:
                emit_C_group(g)
        if PH < 3:
            sch.finalize(nc, es)
            return nc

        dma_pool(lambda e: e.dma_start(
            out=sE[:, 0:8192].rearrange("p (kc d) -> p kc d", kc=8),
            in_=wqk_h[:, D:2 * D].rearrange("(kc p) d -> p kc d", p=128)),
            ["wWk", "wV1"], tag="ldWk")

        for g in range(NG):
            t0 = g * 512
            xn_keys = [f"xn.{4 * g + i}" for i in range(4)]
            for fg in range(8):
                bank = pb[fg % 4]
                for kc in range(8):
                    def r_mm(e, fg=fg, kc=kc, t0=t0, bank=bank):
                        return e.matmul(bank[:, :],
                                        sE[:, kc * 1024 + fg * 128:kc * 1024 + (fg + 1) * 128],
                                        xn(kc, t0, 512),
                                        start=(kc == 0), stop=(kc == 7))
                    sch.add("pe", r_mm, reads=["wWk"] + xn_keys, writes=[f"pb{fg % 4}"],
                            tag=f"k{g}.{fg}.{kc}")

                def r_ev(e, fg=fg, g=g, bank=bank):
                    return e.tensor_scalar(sC[:, fg * 2048 + g * 512:fg * 2048 + (g + 1) * 512],
                                           bank[:, :], bqk_sb[:, 8 + fg:9 + fg], None,
                                           ALU.add)
                kw = [f"kfm.{fg}.{g}"]
                if fg == 0:
                    kw += ["wU1"]
                elif fg == 1:
                    kw += ["P.0", "P.1", "Hs.0", "Hs.1"]
                elif fg == 2:
                    kw += ["Qs.0", "Qs.1", "wV2"]
                elif fg == 3:
                    kw += ["wV2", "Hs.2", "Hs.3"]
                sch.add("dve", r_ev, reads=[f"pb{fg % 4}", "bqk"], writes=kw,
                        tag=f"kev{g}.{fg}")

            def emit_trB(c):
                for kc in range(8):
                    def r_tr(e, c=c, kc=kc):
                        return e.transpose(ptA[:, kc * 128:(kc + 1) * 128],
                                           sC[:, kc * 2048 + c * 128:kc * 2048 + (c + 1) * 128],
                                           smb[:, 512:640])
                    sch.add("pe", r_tr, reads=[f"kfm.{kc}.{g}", "ident"],
                            writes=["ptA"], tag=f"trB{c}.{kc}")

                kcol = KTM0 + (c % 2) * 1024

                def r_ev(e, kcol=kcol):
                    return e.tensor_copy(sE[:, kcol:kcol + 1024], ptA[:, :])
                sch.add("dve", r_ev, reads=["ptA"], writes=[f"ktm.{c % 2}", "wU2"],
                        tag=f"ktmB{c}")

            def emit_stB(c):
                kcol = KTM0 + (c % 2) * 1024
                for h in range(H):
                    p0 = (h % 2) * 64
                    colS = (h // 2) * 64

                    def r_mm(e, c=c, h=h, p0=p0, colS=colS, kcol=kcol):
                        return e.matmul(pb[6][p0:p0 + 64, colS:colS + 64],
                                        sE[:, kcol + h * 64:kcol + h * 64 + 64],
                                        sD[:, c * 1024 + h * 64:c * 1024 + h * 64 + 64],
                                        start=(c == 0 and h <= 1),
                                        stop=(c == NT - 1 and h == H - 1),
                                        skip_group_check=True)
                    sch.add("pe", r_mm,
                            reads=[f"ktm.{c % 2}", f"vtm.{c}.{h // 8}"],
                            writes=["pb6"], tag=f"stB{c}.{h}")

            emit_trB(4 * g)
            for ci in range(1, 4):
                emit_trB(4 * g + ci)
                emit_stB(4 * g + ci - 1)
            emit_stB(4 * g + 3)

        dma_pool(lambda e: e.dma_start(
            out=wVQ[:, :].rearrange("p (kc d) -> p kc d", kc=8),
            in_=wqk_h[:, 0:D].rearrange("(kc p) d -> p kc d", p=128)),
            ["wWq", "wWv"], tag="ldWq")

        sch.add("dve", lambda e: e.tensor_copy(fscr[:, SLOC0:SLOC0 + 512], pb[6][:, :]),
                reads=["pb6"], writes=["S_loc"], tag="slocev")
        sch.add("dve", lambda e: e.tensor_scalar(fscr[:, SLOC0:SLOC0 + 512],
                                                 fscr[:, SLOC0:SLOC0 + 512],
                                                 lnscr[:, 128:129], None, ALU.mult),
                reads=["S_loc", "uv"], writes=["S_loc"], tag="slocu")
        dma_pool(lambda e: e.dma_start(out=cc_in, in_=fscr[:, SLOC0:SLOC0 + 512]),
                 ["cc_in"], reads=["S_loc"], tag="ccout")
        sch.add("pool", lambda e: e.collective_compute(
            "AllReduce", ALU.add,
            replica_groups=[[2 * i, 2 * i + 1] for i in range(N_CORES // 2)],
            ins=[cc_in.opt()], outs=[cc_out.opt()]),
            reads=["cc_in"], writes=["cc_done"], kind="cc", tag="allreduce")

        def emit_sinit():
            dma_pool(lambda e: e.dma_start(out=fscr[:, GATH0:GATH0 + 512],
                                           in_=cc_out),
                     ["gath"], reads=["cc_done", "dsq"], tag="gathin",
                     dsem="gath", dmode="serial")
            sch.add("dve", lambda e: e.tensor_scalar(srun[:, :],
                                                     fscr[:, GATH0:GATH0 + 512],
                                                     lnscr[:, 129:130], None,
                                                     ALU.mult),
                    reads=["gath", "uv"], writes=["S_run"], tag="sinit")
            sch.add("act", lambda e: e.copy(smb[:, SRBF0:SRBF0 + 512], srun[:, :]),
                    reads=["S_run"], writes=["S_runbf"], tag="srbf0")

        if PH < 4:
            sch.finalize(nc, es)
            return nc
        DE_G = int(_os.environ.get("K_DE_G", "4"))
        DE_LVL = int(_os.environ.get("K_DE_LVL", "9"))

        def emit_D_group(g):
            t0 = g * 512
            par = g % 2
            xn_keys = [f"xn.{4 * g + i}" for i in range(4)]
            for kc in range(8):
                def r_mm(e, kc=kc, t0=t0):
                    return e.matmul(pb[5][0:16, :],
                                    smb[:, WN0 + kc * 16:WN0 + (kc + 1) * 16],
                                    xn(kc, t0, 512),
                                    start=(kc == 0), stop=(kc == 7),
                                    skip_group_check=True)
                sch.add("pe", r_mm, reads=["wWn"] + xn_keys, writes=["pb5"],
                        tag=f"n{g}.{kc}")

            def r_env(e, t0=t0):
                return e.activation(smb[0:16, EN0 + t0:EN0 + t0 + 512], pb[5][0:16, :],
                                    AF.Exp, bias=bn_sb[0:16, 1:2], scale=-1.0)
            sch.add("act", r_env, reads=["pb5", "bnneg"], writes=[f"en.{g}"],
                    tag=f"env{g}")

            for fg in range(8):
                fb = pb[3 + fg % 2]

                def r_fmm(e, fg=fg, t0=t0, fb=fb):
                    return e.matmul(fb[:, :],
                                    smb[0:16, SEL0 + fg * 128:SEL0 + (fg + 1) * 128],
                                    smb[0:16, EN0 + t0:EN0 + t0 + 512],
                                    start=True, stop=True)
                sch.add("pe", r_fmm, reads=["sel", f"en.{g}"], writes=[f"pb{3 + fg % 2}"],
                        tag=f"fac{g}.{fg}")

                fcol = FACT0 if fg % 2 == 0 else 1536

                def r_fev(e, fb=fb, fcol=fcol):
                    return e.tensor_copy(fscr[:, fcol:fcol + 512], fb[:, :])
                sch.add("dve", r_fev, reads=[f"pb{3 + fg % 2}", "dsq"],
                        writes=[f"factor.{fg % 2}"], tag=f"facev{g}.{fg}")

                qb = pb[fg % 3]
                for kc in range(8):
                    def r_mm(e, fg=fg, kc=kc, t0=t0, qb=qb):
                        return e.matmul(qb[:, :],
                                        wVQ[:, kc * 1024 + fg * 128:kc * 1024 + (fg + 1) * 128],
                                        xn(kc, t0, 512),
                                        start=(kc == 0), stop=(kc == 7))
                    sch.add("pe", r_mm, reads=["wWq"] + xn_keys, writes=[f"pb{fg % 3}"],
                            tag=f"q{g}.{fg}.{kc}")

                qcol = QB0 + par * 4096 + fg * 512

                fcol2 = FACT0 if fg % 2 == 0 else 1536

                def r_qev(e, fg=fg, qcol=qcol, qb=qb, fcol2=fcol2):
                    return e.scalar_tensor_tensor(sE[:, qcol:qcol + 512], qb[:, :],
                                                  bqk_sb[:, fg:fg + 1],
                                                  fscr[:, fcol2:fcol2 + 512],
                                                  ALU.add, ALU.mult)
                sch.add("dve", r_qev, reads=[f"pb{fg % 3}", "bqk", f"factor.{fg % 2}"],
                        writes=[f"q.{par}.{fg}", "wWk"], tag=f"qev{g}.{fg}")

        def emit_E_group(g):
            par = g % 2
            for ci in range(4 if DE_LVL >= 1 else 0):
                c = 4 * g + ci
                crel = ci * 128
                for h in range(H):
                    bank = pb[h % 4]
                    p0 = (h % 2) * 64
                    fg = h // 2
                    qcol = QB0 + par * 4096 + fg * 512 + crel

                    def r_mm(e, h=h, c=c, p0=p0, fg=fg, qcol=qcol, bank=bank):
                        return e.matmul(bank[:, (h // 4) * 128:(h // 4 + 1) * 128],
                                        sC[p0:p0 + 64, fg * 2048 + c * 128:fg * 2048 + (c + 1) * 128],
                                        sE[p0:p0 + 64, qcol:qcol + 128],
                                        start=(h < 4), stop=(h >= 12),
                                        skip_group_check=True)
                    sch.add("pe", r_mm,
                            reads=[f"kfm.{fg}.{g}", f"q.{par}.{fg}"],
                            writes=[f"pb{h % 4}"], tag=f"sc{c}.{h}")

                for bk in range(4 if DE_LVL >= 2 else 0):
                    def r_ev(e, bk=bk):
                        out_ap = sE[:, SC0:SC0 + 2048] \
                            .rearrange("p (j b n) -> p j b n", b=4, n=128)[:, :, bk, :]
                        return e.tensor_tensor(out_ap,
                                               pb[bk][:, :].rearrange("p (j n) -> p j n", n=128),
                                               smb[:, 0:512].rearrange("p (j n) -> p j n", n=128),
                                               ALU.mult)
                    sch.add("dve", r_ev, reads=[f"pb{bk}", "mask"],
                            writes=[f"scs.{bk}", "wU2"], tag=f"scev{c}.{bk}")

                if DE_LVL >= 5:
                    for kc in range(8):
                        def r_tr(e, c=c, kc=kc):
                            return e.transpose(ptA[:, kc * 128:(kc + 1) * 128],
                                               sC[:, kc * 2048 + c * 128:kc * 2048 + (c + 1) * 128],
                                               smb[:, 512:640])
                        sch.add("pe", r_tr, reads=[f"kfm.{kc}.{g}", "ident"],
                                writes=["ptA"], tag=f"trE{c}.{kc}")

                    kcol = KTM0 + (c % 2) * 1024

                    def r_kev(e, kcol=kcol):
                        return e.copy(sE[:, kcol:kcol + 1024], ptA[:, :])
                    sch.add("act", r_kev, reads=["ptA"], writes=[f"ktm.{c % 2}", "wU2"],
                            tag=f"ktmE{c}")

                    for h in range(H):
                        p0 = (h % 2) * 64
                        colS = (h // 2) * 64

                        def r_mm(e, c=c, h=h, p0=p0, colS=colS, kcol=kcol):
                            return e.matmul(pb[6][p0:p0 + 64, colS:colS + 64],
                                            sE[:, kcol + h * 64:kcol + h * 64 + 64],
                                            sD[:, c * 1024 + h * 64:c * 1024 + h * 64 + 64],
                                            start=(h <= 1), stop=(h >= H - 2),
                                            skip_group_check=True)
                        sch.add("pe", r_mm,
                                reads=[f"ktm.{c % 2}", f"vtm.{c}.{h // 8}"],
                                writes=["pb6"], tag=f"stE{c}.{h}")
                for h in range(H if DE_LVL >= 3 else 0):
                    j = h // 2
                    bank = pb[4 + j // 4]
                    p0 = (h % 2) * 64
                    colc = (j % 4) * 128
                    qcol = QB0 + par * 4096 + (h // 2) * 512 + crel

                    def r_inter(e, h=h, p0=p0, colc=colc, qcol=qcol, bank=bank):
                        return e.matmul(bank[p0:p0 + 64, colc:colc + 128],
                                        smb[p0:p0 + 64, SRBF0 + (h // 2) * 64:SRBF0 + (h // 2) * 64 + 64],
                                        sE[p0:p0 + 64, qcol:qcol + 128],
                                        start=(h % 8 <= 1), stop=False,
                                        skip_group_check=True)
                    sch.add("pe", r_inter,
                            reads=["S_runbf", f"q.{par}.{h // 2}"],
                            writes=[f"pb{4 + j // 4}"], tag=f"cx{c}.{h}")
                for h in range(H if DE_LVL >= 3 else 0):
                    j = h // 2
                    bank = pb[4 + j // 4]
                    p0 = (h % 2) * 64
                    colc = (j % 4) * 128

                    def r_intra(e, h=h, c=c, p0=p0, colc=colc, bank=bank, j=j):
                        return e.matmul(bank[p0:p0 + 64, colc:colc + 128],
                                        sD[:, c * 1024 + h * 64:c * 1024 + h * 64 + 64],
                                        sE[:, SC0 + h * 128:SC0 + (h + 1) * 128],
                                        start=False, stop=(h % 8 >= 6),
                                        skip_group_check=True)
                    sch.add("pe", r_intra,
                            reads=[f"vtm.{c}.{h // 8}", f"scs.{h % 4}"],
                            writes=[f"pb{4 + j // 4}"], tag=f"ci{c}.{h}")

                if DE_LVL < 4:
                    continue
                def r_ev0(e, c=c):
                    return e.copy(
                        sA[:, :].rearrange("p (j cc) -> p j cc", j=8)
                          [:, 0:4, c * 128:(c + 1) * 128],
                        pb[4][:, :].rearrange("p (j n) -> p j n", j=4))
                sch.add("act", r_ev0, reads=["pb4"],
                        writes=[f"ctx.0.{c}", f"xn.{c}"], tag=f"cev{c}.0")

                def r_ev1(e, c=c):
                    return e.tensor_copy(
                        sA[:, :].rearrange("p (j cc) -> p j cc", j=8)
                          [:, 4:8, c * 128:(c + 1) * 128],
                        pb[5][:, :].rearrange("p (j n) -> p j n", j=4))
                sch.add("dve", r_ev1, reads=["pb5"],
                        writes=[f"ctx.1.{c}", f"xn.{c}"], tag=f"cev{c}.1")

                if DE_LVL < 5:
                    continue

                def r_supd(e):
                    return e.tensor_tensor(srun[:, :], srun[:, :], pb[6][:, :], ALU.add)
                sch.add("dve", r_supd, reads=["S_run", "pb6"], writes=["S_run"],
                        tag=f"supd{c}")

                def r_sbf(e):
                    return e.copy(smb[:, SRBF0:SRBF0 + 512], srun[:, :])
                sch.add("act", r_sbf, reads=["S_run"], writes=["S_runbf"],
                        tag=f"sbf{c}")

        _de_groups = min(NG, DE_G)
        emit_D_group(0)
        if _de_groups > 1:
            emit_D_group(1)
        emit_sinit()
        for g in range(_de_groups):
            emit_E_group(g)
            if g + 2 < _de_groups:
                emit_D_group(g + 2)
                if g + 2 == _de_groups - 1 and PH >= 5:
                    dma_pool(lambda e: e.dma_start(
                        out=wVQ[:, :].rearrange("p (jc d) -> p jc d", jc=8),
                        in_=wo_h.rearrange("(jc p) d -> p jc d", p=128)),
                        ["wWo", "wWq"], tag="ldWo")

        if PH < 5:
            sch.finalize(nc, es)
            return nc
        if not (_de_groups == NG and _de_groups >= 3):
            dma_pool(lambda e: e.dma_start(
                out=wVQ[:, :].rearrange("p (jc d) -> p jc d", jc=8),
                in_=wo_h.rearrange("(jc p) d -> p jc d", p=128)),
                ["wWo", "wWq"], tag="ldWo")

        for t in range(NT):
            buf = t % 3
            obuf = t % 2

            def r_x(e, t=t, buf=buf):
                return e.dma_start(out=xf32[:, buf * 1024:(buf + 1) * 1024],
                                   in_=x_h[t * 128:(t + 1) * 128, :])
            dma_sp(r_x, [f"xf.{buf}"], tag=f"ldxF{t}", dsem=f"xf{buf}", dmode="serial")

            for hf in range(2):
                bk = (t % 2) * 2 + hf
                bank = pb[bk]
                for jc in range(8):
                    def r_mm(e, t=t, hf=hf, jc=jc, bank=bank):
                        return e.matmul(bank[:, :],
                                        sA[:, jc * 2048 + t * 128:jc * 2048 + (t + 1) * 128],
                                        wVQ[:, jc * 1024 + hf * 512:jc * 1024 + (hf + 1) * 512],
                                        start=(jc == 0), stop=False)
                    sch.add("pe", r_mm,
                            reads=["wWo", f"ctx.0.{t}", f"ctx.1.{t}"],
                            writes=[f"pb{bk}"], tag=f"wo{t}.{hf}.{jc}")

                def r_bias(e, hf=hf, bank=bank):
                    return e.matmul(bank[:, :],
                                    smb[32:33, ONES0:ONES0 + 128],
                                    smb[32:33, BOR + hf * 512:BOR + (hf + 1) * 512],
                                    start=False, stop=True)
                sch.add("pe", r_bias, reads=["ones", "bor"], writes=[f"pb{bk}"],
                        tag=f"wob{t}.{hf}")

                ocol = obuf * 1024 + hf * 512

                def r_m1(e, t=t, hf=hf, ocol=ocol, bank=bank):
                    return e.tensor_tensor(fscr[:, ocol:ocol + 512], bank[:, :],
                                           sB[:, t * 1024 + hf * 512:t * 1024 + (hf + 1) * 512],
                                           ALU.mult)
                m1w = [f"ost.{obuf}.{hf}"]
                if t < 2:
                    m1w += ["factor.0", "factor.1", "gath", "S_loc"]
                sch.add("dve", r_m1, reads=[f"pb{bk}", f"g.{t}.{hf}"], writes=m1w,
                        tag=f"m1.{t}.{hf}")

                def r_add(e, t=t, hf=hf, ocol=ocol, buf=buf):
                    return e.tensor_tensor(fscr[:, ocol:ocol + 512],
                                           fscr[:, ocol:ocol + 512],
                                           xf32[:, buf * 1024 + hf * 512:buf * 1024 + (hf + 1) * 512],
                                           ALU.add)
                sch.add("pool", r_add, reads=[f"ost.{obuf}.{hf}", f"xf.{buf}"],
                        writes=[f"ost.{obuf}.{hf}"], tag=f"oadd.{t}.{hf}")

            def r_st(e, t=t, obuf=obuf):
                return e.dma_start(out=out_h[t * 128:(t + 1) * 128, :],
                                   in_=fscr[:, obuf * 1024:(obuf + 1) * 1024])
            dma_sp(r_st, [f"st.{t}"],
                   reads=[f"ost.{obuf}.0", f"ost.{obuf}.1"], tag=f"st{t}",
                   dsem=f"st{obuf}", dmode="serial")

        sch.finalize(nc, es)
    return nc


_NC_CACHE = {}


def _get_program():
    if "nc" not in _NC_CACHE:
        _NC_CACHE["nc"] = build_program()
    return _NC_CACHE["nc"]


def kernel(L0_attn_x, L0_attn_Wqk, L0_attn_bqk, L0_attn_Wv, L0_attn_bv,
           L0_attn_Wn, L0_attn_bn, L0_attn_Wo, L0_attn_bo,
           L0_ffn_U1, L0_ffn_V1, L0_ffn_b1, L0_ffn_U2, L0_ffn_V2, L0_ffn_b2):
    x = np.ascontiguousarray(np.asarray(L0_attn_x, dtype=np.float32))
    f32 = lambda a: np.ascontiguousarray(np.asarray(a, dtype=np.float32))
    bf16 = ml_dtypes.bfloat16

    Wqk, Wv, Wn, Wo = f32(L0_attn_Wqk), f32(L0_attn_Wv), f32(L0_attn_Wn), f32(L0_attn_Wo)
    U1, V1, U2, V2 = f32(L0_ffn_U1), f32(L0_ffn_V1), f32(L0_ffn_U2), f32(L0_ffn_V2)
    bqk, bn, bv, bo = f32(L0_attn_bqk), f32(L0_attn_bn), f32(L0_attn_bv), f32(L0_attn_bo)
    b1, b2 = f32(L0_ffn_b1), f32(L0_ffn_b2)

    bqk_t = np.ascontiguousarray(bqk.reshape(16, 128).T)
    bn_t = np.ascontiguousarray(bn.reshape(16, 1))
    b1_t = np.ascontiguousarray(b1.reshape(32, 128).T)
    bv_r = bv.reshape(1, D)
    bo_r = bo.reshape(1, D)
    b2_r = b2.reshape(1, D)

    tril = np.tril(np.ones((128, 128), np.float32)).T
    mask4 = np.ascontiguousarray(np.tile(tril, (1, 4))).astype(bf16)
    ident = np.eye(128, dtype=np.float32).astype(bf16)
    sel = np.zeros((16, 1024), np.float32)
    for fg in range(8):
        for p in range(128):
            sel[2 * fg + p // 64, fg * 128 + p] = 1.0
    sel = sel.astype(bf16)
    ones_r = np.ones((1, 128), np.float32).astype(bf16)

    nc = _get_program()
    in_maps = []
    for c in range(N_CORES):
        b, half = c // 2, c % 2
        xs = np.ascontiguousarray(x[b, half * TOK:(half + 1) * TOK, :])
        uv = np.zeros((128, 2), np.float32)
        uv[:, 0] = 1.0 - half
        uv[:, 1] = half
        in_maps.append({
            "x": xs, "Wqk": Wqk, "Wv": Wv, "Wn": Wn, "Wo": Wo,
            "U1": U1, "V1": V1, "U2": U2, "V2": V2,
            "bqk_t": bqk_t, "bn_t": bn_t, "b1_t": b1_t,
            "bv_r": bv_r, "bo_r": bo_r, "b2_r": b2_r,
            "mask4": mask4, "ident": ident, "sel": sel, "ones_r": ones_r,
            "uv": uv,
        })

    res = run_bass_kernel_spmd(nc, in_maps, core_ids=list(range(N_CORES)))
    out = np.empty((B, S, D), np.float32)
    for c in range(N_CORES):
        b, half = c // 2, c % 2
        out[b, half * TOK:(half + 1) * TOK, :] = res.results[c]["out"]
    return out


# version 18
# speedup vs baseline: 1.0076x; 1.0076x over previous
import numpy as np
import ml_dtypes
from contextlib import ExitStack

import concourse.bass as bass
import concourse.mybir as mybir
from concourse.bass_utils import run_bass_kernel_spmd

F32 = mybir.dt.float32
BF16 = mybir.dt.bfloat16
AF = mybir.ActivationFunctionType
ALU = mybir.AluOpType

B, S, D = 4, 4096, 1024
H, DH = 16, 64
FF, R = 4096, 256
TOK = 2048
NT = 16
NG = 4
EPS = 1e-5
N_CORES = 8


class _Op:
    __slots__ = ("eng", "emit", "deps", "marked", "kind", "tag", "handle", "pos",
                 "dsem")

    def __init__(self, eng, emit, kind, tag):
        self.eng, self.emit, self.kind, self.tag = eng, emit, kind, tag
        self.deps = []
        self.marked = False
        self.handle = None
        self.pos = 0
        self.dsem = None


class Sched:

    def __init__(self):
        self.streams = {e: [] for e in ("pe", "act", "dve", "pool", "sp")}
        self.last_w = {}
        self.readers = {}
        self.dgroups = {}

    def add(self, eng, emit, reads=(), writes=(), kind="c", tag="", dsem=None,
            dmode="batch"):
        op = _Op(eng, emit, kind, tag)
        if kind == "d":
            if dsem is None:
                dsem, dmode = f"_d_{tag or len(self.dgroups)}", "batch"
            op.dsem = dsem
            self.dgroups.setdefault(dsem, [dmode, []])[1].append(op)
            self.dgroups[dsem][0] = dmode
        op.pos = len(self.streams[eng])
        deps = set()
        for k in reads:
            w = self.last_w.get(k)
            if w is not None:
                deps.add(w)
        for k in writes:
            w = self.last_w.get(k)
            if w is not None:
                deps.add(w)
            for r in self.readers.get(k, ()):
                deps.add(r)
        for d in deps:
            if d is op:
                continue
            if (d.eng == eng and d.kind == "c" and kind == "c" and eng == "pe"):
                continue
            op.deps.append(d)
        self.streams[eng].append(op)
        for k in writes:
            self.last_w[k] = op
            self.readers[k] = []
        for k in reads:
            self.readers.setdefault(k, []).append(op)
        return op

    def finalize(self, nc, es):
        sems = {}
        for e in ("pe", "act", "dve", "pool"):
            sems[e] = es.enter_context(nc.semaphore(f"sem_{e}"))
        sems["cc"] = es.enter_context(nc.semaphore("sem_cc"))
        for gname in self.dgroups:
            sems[gname] = es.enter_context(nc.semaphore(f"sem_{gname}"))

        for ops in self.streams.values():
            for op in ops:
                for d in op.deps:
                    d.marked = True
        self._dfinal = {}
        for gname, (mode, gops) in self.dgroups.items():
            total = 16 * len(gops)
            for i, op in enumerate(gops):
                op.handle = (gname, 16 * (i + 1) if mode == "serial" else total)
                op.marked = True
            self._dfinal[gname] = total
        for eng, ops in self.streams.items():
            nc_, ncc = 0, 0
            for op in ops:
                if op.kind == "cc":
                    ncc += 1
                    op.handle = ("cc", ncc)
                    op.marked = True
                elif op.kind == "c" and op.marked:
                    nc_ += 1
                    op.handle = (eng, nc_)

        block = es.enter_context(nc.Block())
        engmap = {"pe": block.tensor, "act": block.scalar, "dve": block.vector,
                  "pool": block.gpsimd, "sp": block.sync}
        sched = self

        def emit_stream(eng_name):
            def body(eng):
                waited = {}
                for op in sched.streams[eng_name]:
                    for d in sorted(op.deps, key=lambda o: (o.handle[0], o.handle[1])):
                        sem_name, val = d.handle
                        if waited.get(sem_name, 0) < val:
                            eng.wait_ge(sems[sem_name], val)
                            waited[sem_name] = val
                    inst = op.emit(eng)
                    if op.marked:
                        sem_name, val = op.handle
                        inst.then_inc(sems[sem_name], 16 if op.kind == "d" else 1)
                if eng_name == "sp":
                    for gname, total in sorted(sched._dfinal.items()):
                        if waited.get(gname, 0) < total:
                            eng.wait_ge(sems[gname], total)
            return body

        for name, deco in engmap.items():
            deco(emit_stream(name))


def build_program():
    import os as _os
    PH = int(_os.environ.get("K_PHASES", "5"))
    nc = bass.Bass("TRN2", target_bir_lowering=False)

    x_h = nc.dram_tensor("x", [TOK, D], F32, kind="ExternalInput").ap()
    wqk_h = nc.dram_tensor("Wqk", [D, 2 * D], F32, kind="ExternalInput").ap()
    wv_h = nc.dram_tensor("Wv", [D, D], F32, kind="ExternalInput").ap()
    wn_h = nc.dram_tensor("Wn", [D, H], F32, kind="ExternalInput").ap()
    wo_h = nc.dram_tensor("Wo", [D, D], F32, kind="ExternalInput").ap()
    u1_h = nc.dram_tensor("U1", [D, R], F32, kind="ExternalInput").ap()
    v1_h = nc.dram_tensor("V1", [R, FF], F32, kind="ExternalInput").ap()
    u2_h = nc.dram_tensor("U2", [FF, R], F32, kind="ExternalInput").ap()
    v2_h = nc.dram_tensor("V2", [R, D], F32, kind="ExternalInput").ap()
    bqk_h = nc.dram_tensor("bqk_t", [128, 16], F32, kind="ExternalInput").ap()
    bn_h = nc.dram_tensor("bn_t", [16, 1], F32, kind="ExternalInput").ap()
    b1_h = nc.dram_tensor("b1_t", [128, 32], F32, kind="ExternalInput").ap()
    bv_h = nc.dram_tensor("bv_r", [1, D], F32, kind="ExternalInput").ap()
    bo_h = nc.dram_tensor("bo_r", [1, D], F32, kind="ExternalInput").ap()
    b2_h = nc.dram_tensor("b2_r", [1, D], F32, kind="ExternalInput").ap()
    mask_h = nc.dram_tensor("mask4", [128, 512], BF16, kind="ExternalInput").ap()
    ident_h = nc.dram_tensor("ident", [128, 128], BF16, kind="ExternalInput").ap()
    sel_h = nc.dram_tensor("sel", [16, 1024], BF16, kind="ExternalInput").ap()
    ones_h = nc.dram_tensor("ones_r", [1, 128], BF16, kind="ExternalInput").ap()
    uv_h = nc.dram_tensor("uv", [128, 2], F32, kind="ExternalInput").ap()
    out_h = nc.dram_tensor("out", [TOK, D], F32, kind="ExternalOutput").ap()

    cc_in = nc.dram_tensor("cc_in", [128, 512], F32, kind="Internal").ap()
    cc_out = nc.dram_tensor("cc_out", [128, 512], F32, kind="Internal").ap()

    es = ExitStack()
    with es:
        sA = es.enter_context(nc.sbuf_tensor("sA", [128, 16384], BF16))
        sB = es.enter_context(nc.sbuf_tensor("sB", [128, 16384], BF16))
        sC = es.enter_context(nc.sbuf_tensor("sC", [128, 16384], BF16))
        sD = es.enter_context(nc.sbuf_tensor("sD", [128, 16384], BF16))
        sE = es.enter_context(nc.sbuf_tensor("sE", [128, 16384], BF16))
        wVQ = es.enter_context(nc.sbuf_tensor("wVQ", [128, 8192], BF16))
        smb = es.enter_context(nc.sbuf_tensor("smb", [128, 4480], BF16))
        xf32 = es.enter_context(nc.sbuf_tensor("xf32", [128, 3072], F32))
        lnscr = es.enter_context(nc.sbuf_tensor("lnscr", [128, 136], F32))
        fscr = es.enter_context(nc.sbuf_tensor("fscr", [128, 2048], F32))
        srun = es.enter_context(nc.sbuf_tensor("srun", [128, 512], F32))
        bqk_sb = es.enter_context(nc.sbuf_tensor("bqk_sb", [128, 16], F32))
        b1_sb = es.enter_context(nc.sbuf_tensor("b1_sb", [128, 32], F32))
        bn_sb = es.enter_context(nc.sbuf_tensor("bn_sb", [16, 2], F32))
        pb = [es.enter_context(nc.psum_tensor(f"pb{i}", [128, 512], F32))
              for i in range(7)]
        ptA = es.enter_context(nc.psum_tensor("ptA", [128, 1024], BF16))

        def xn(kc, t0, w):
            return sA[:, kc * 2048 + t0:kc * 2048 + t0 + w]
        XNSCR = (sB, 0)
        PSB = (sC, 2048)
        HSCR = (sC, 3072)
        QSB = (sC, 4096)
        WK0 = 0
        QB0 = 0
        KTM0 = 8192
        SC0 = 10240
        MASK = smb[:, 0:512]
        IDENT = smb[:, 512:640]
        EN0 = 640
        SEL0 = 2688
        BVR, BOR, B2R = 640, 1664, 2688
        SRBF0 = 3712
        ONES0 = 4224
        WN0 = 4352
        FACT0, GATH0, SLOC0 = 0, 512, 1024
        SQD = 0

        sch = Sched()

        def dma_pool(emit, writes, reads=(), tag="", dsem=None, dmode="batch"):
            sch.add("pool", emit, reads=reads, writes=writes, kind="d", tag=tag,
                    dsem=dsem, dmode=dmode)

        def dma_sp(emit, writes, reads=(), tag="", dsem=None, dmode="batch"):
            sch.add("sp", emit, reads=reads, writes=writes, kind="d", tag=tag,
                    dsem=dsem, dmode=dmode)

        dma_sp(lambda e: e.dma_start(out=smb[:, 0:512], in_=mask_h), ["mask"], dsem="spsm")
        dma_sp(lambda e: e.dma_start(out=smb[:, 512:640], in_=ident_h), ["ident"], dsem="spsm")
        dma_sp(lambda e: e.dma_start(out=smb[0:16, SEL0:SEL0 + 1024], in_=sel_h), ["sel"], dsem="spsm")
        dma_sp(lambda e: e.dma_start(out=smb[32:33, ONES0:ONES0 + 128], in_=ones_h), ["ones"], dsem="spsm")
        dma_sp(lambda e: e.dma_start(out=bqk_sb[:, :], in_=bqk_h), ["bqk"], dsem="spsm")
        dma_sp(lambda e: e.dma_start(out=b1_sb[:, :], in_=b1_h), ["b1"], dsem="spsm")
        dma_sp(lambda e: e.dma_start(out=bn_sb[:, 0:1], in_=bn_h), ["bn"], dsem="spsm")
        dma_sp(lambda e: e.dma_start(out=lnscr[:, 128:130], in_=uv_h), ["uv"], dsem="spsm")
        dma_pool(lambda e: e.dma_start(
            out=wVQ[:, :].rearrange("p (kc d) -> p kc d", kc=8),
            in_=wv_h.rearrange("(kc p) d -> p kc d", p=128)), ["wWv"], tag="ldWv")
        dma_pool(lambda e: e.dma_start(
            out=sC[:, 0:2048].rearrange("p (kc r) -> p kc r", kc=8),
            in_=u1_h.rearrange("(kc p) r -> p kc r", p=128)), ["wU1"], tag="ldU1")
        dma_pool(lambda e: e.dma_start(
            out=sE[:, 0:8192].rearrange("p (rc f) -> p rc f", rc=2),
            in_=v1_h.rearrange("(rc p) f -> p rc f", p=128)), ["wV1"], tag="ldV1")
        dma_pool(lambda e: e.dma_start(
            out=sE[:, 8192:16384].rearrange("p (fc r) -> p fc r", fc=32),
            in_=u2_h.rearrange("(fc p) r -> p fc r", p=128)), ["wU2"], tag="ldU2")
        dma_pool(lambda e: e.dma_start(
            out=sC[:, 5120:7168].rearrange("p (rc d) -> p rc d", rc=2),
            in_=v2_h.rearrange("(rc p) d -> p rc d", p=128)), ["wV2"], tag="ldV2")
        dma_pool(lambda e: e.dma_start(out=smb[32:33, B2R:B2R + 1024], in_=b2_h), ["b2r"], dsem="pw")
        dma_pool(lambda e: e.dma_start(out=smb[32:33, BVR:BVR + 1024], in_=bv_h), ["bvr"], dsem="pw")
        dma_pool(lambda e: e.dma_start(out=smb[32:33, BOR:BOR + 1024], in_=bo_h), ["bor"], dsem="pw")
        dma_pool(lambda e: e.dma_start(
            out=smb[:, WN0:WN0 + 128].rearrange("p (kc h) -> p kc h", kc=8),
            in_=wn_h.rearrange("(kc p) h -> p kc h", p=128)),
            ["wWn"], tag="ldWn", dsem="pw")
        sch.add("dve", lambda e: e.tensor_scalar(bn_sb[0:16, 1:2], bn_sb[0:16, 0:1],
                                                 -1.0, None, ALU.mult),
                reads=["bn"], writes=["bnneg"], tag="bnneg")
        sch.add("dve", lambda e: e.memset(lnscr[:, 130:131], EPS),
                reads=(), writes=["epsc"], tag="epsset")

        def emit_A_ln(t):
            buf = t % 3

            def em(e, t=t, buf=buf):
                return e.dma_start(out=xf32[:, buf * 1024:(buf + 1) * 1024],
                                   in_=x_h[t * 128:(t + 1) * 128, :])
            dma_sp(em, [f"xf.{buf}"], tag=f"ldx{t}", dsem=f"xf{buf}", dmode="serial")

            c0 = t * 8

            def r_sum(e, t=t, buf=buf, c0=c0):
                return e.tensor_reduce(lnscr[:, c0:c0 + 1],
                                       xf32[:, buf * 1024:(buf + 1) * 1024],
                                       mybir.AxisListType.X, ALU.add, negate=True)
            sch.add("dve", r_sum, reads=[f"xf.{buf}"], writes=[f"ln.{t}"], tag=f"lnsum{t}")

            def r_sq(e, t=t, buf=buf, c0=c0):
                return e.activation(fscr[:, SQD:SQD + 1024],
                                    xf32[:, buf * 1024:(buf + 1) * 1024],
                                    AF.Square, accum_out=lnscr[:, c0 + 1:c0 + 2])
            sch.add("act", r_sq, reads=[f"xf.{buf}"], writes=["dsq", f"lnsq.{t}"], tag=f"lnsq{t}")

            def r_m(e, c0=c0):
                return e.tensor_scalar(lnscr[:, c0 + 2:c0 + 3], lnscr[:, c0:c0 + 1],
                                       1.0 / D, None, ALU.mult)
            sch.add("dve", r_m, reads=[f"ln.{t}"], writes=[f"lnm.{t}"], tag=f"lnm{t}")

            def r_msq(e, c0=c0):
                return e.tensor_tensor(lnscr[:, c0 + 3:c0 + 4], lnscr[:, c0 + 2:c0 + 3],
                                       lnscr[:, c0 + 2:c0 + 3], ALU.mult)
            sch.add("dve", r_msq, reads=[f"lnm.{t}"], writes=[f"lnmsq.{t}"], tag=f"lnmsq{t}")

            def r_var(e, c0=c0):
                return e.scalar_tensor_tensor(lnscr[:, c0 + 4:c0 + 5],
                                              lnscr[:, c0 + 1:c0 + 2], 1.0 / D,
                                              lnscr[:, c0 + 3:c0 + 4],
                                              ALU.mult, ALU.subtract)
            sch.add("dve", r_var, reads=[f"lnsq.{t}", f"lnmsq.{t}"],
                    writes=[f"lnvar.{t}"], tag=f"lnvar{t}")

            def r_std(e, c0=c0):
                return e.activation(lnscr[:, c0 + 5:c0 + 6], lnscr[:, c0 + 4:c0 + 5],
                                    AF.Sqrt, bias=lnscr[:, 130:131])
            sch.add("act", r_std, reads=[f"lnvar.{t}", "epsc"], writes=[f"lnstd.{t}"],
                    tag=f"lnstd{t}")

            def r_rstd(e, c0=c0):
                return e.reciprocal(lnscr[:, c0 + 6:c0 + 7], lnscr[:, c0 + 5:c0 + 6])
            sch.add("dve", r_rstd, reads=[f"lnstd.{t}"], writes=[f"lnrstd.{t}"], tag=f"lnrstd{t}")

            sbuf_t, scr0 = XNSCR
            sbuf2 = t % 2
            xcol = scr0 + sbuf2 * 1024

            def r_xn(e, t=t, buf=buf, c0=c0, xcol=xcol):
                return e.tensor_scalar(sB[:, xcol:xcol + 1024],
                                       xf32[:, buf * 1024:(buf + 1) * 1024],
                                       lnscr[:, c0 + 2:c0 + 3],
                                       lnscr[:, c0 + 6:c0 + 7],
                                       ALU.add, ALU.mult)
            sch.add("dve", r_xn, reads=[f"xf.{buf}", f"lnm.{t}", f"lnrstd.{t}"],
                    writes=[f"xnscr.{sbuf2}"], tag=f"xn{t}")

        def emit_A_tr(t):
            sbuf2 = t % 2
            xcol = XNSCR[1] + sbuf2 * 1024
            for kc in range(8):
                def r_tr(e, kc=kc, xcol=xcol):
                    return e.transpose(ptA[:, kc * 128:(kc + 1) * 128],
                                       sB[:, xcol + kc * 128:xcol + (kc + 1) * 128],
                                       smb[:, 512:640])
                sch.add("pe", r_tr, reads=[f"xnscr.{sbuf2}", "ident"],
                        writes=["ptA"], tag=f"trA{t}.{kc}")

            def r_ev(e, t=t):
                return e.tensor_copy(
                    sA[:, :].rearrange("p (kc c) -> p kc c", kc=8)[:, :, t * 128:(t + 1) * 128],
                    ptA[:, :].rearrange("p (kc n) -> p kc n", kc=8))
            sch.add("dve", r_ev, reads=["ptA"], writes=[f"xn.{t}"], tag=f"evA{t}")

        def emit_A_tile(t):
            emit_A_ln(t)
            emit_A_tr(t)

        def emit_C_group(g, a_tiles=()):
            t0 = g * 512
            xn_keys = [f"xn.{4 * g + i}" for i in range(4)]
            for rg in range(2):
                bank = pb[rg]
                for kc in range(8):
                    def r_mm(e, rg=rg, kc=kc, t0=t0, bank=bank):
                        return e.matmul(bank[:, :],
                                        sC[:, kc * 256 + rg * 128:kc * 256 + (rg + 1) * 128],
                                        xn(kc, t0, 512),
                                        start=(kc == 0), stop=(kc == 7))
                    sch.add("pe", r_mm, reads=["wU1"] + xn_keys, writes=[f"pb{rg}"],
                            tag=f"P{g}.{rg}.{kc}")

                def r_ev(e, rg=rg, bank=bank):
                    return e.copy(sC[:, PSB[1] + rg * 512:PSB[1] + (rg + 1) * 512], bank[:, :])
                sch.add("act", r_ev, reads=[f"pb{rg}"], writes=[f"P.{rg}"], tag=f"Pev{g}.{rg}")

            def emit_H(fc):
                hb = pb[fc % 4]
                for rc in range(2):
                    def r_mm(e, fc=fc, rc=rc, hb=hb):
                        return e.matmul(hb[:, :],
                                        sE[:, rc * 4096 + fc * 128:rc * 4096 + (fc + 1) * 128],
                                        sC[:, PSB[1] + rc * 512:PSB[1] + (rc + 1) * 512],
                                        start=(rc == 0), stop=(rc == 1))
                    sch.add("pe", r_mm, reads=["wV1", "P.0", "P.1"],
                            writes=[f"pb{fc % 4}"], tag=f"H{g}.{fc}.{rc}")

                hcol = (3072, 3584, 7168, 7680)[fc % 4]

                def r_gelu(e, fc=fc, hb=hb, hcol=hcol):
                    return e.activation(sC[:, hcol:hcol + 512], hb[:, :],
                                        AF.Gelu_apprx_tanh, bias=b1_sb[:, fc:fc + 1])
                sch.add("act", r_gelu, reads=[f"pb{fc % 4}", "b1"],
                        writes=[f"Hs.{fc % 4}"], tag=f"gelu{g}.{fc}")

            def emit_Q(fc):
                hcol = (3072, 3584, 7168, 7680)[fc % 4]
                for rg in range(2):
                    qb = pb[4 + rg]

                    def r_mm(e, fc=fc, rg=rg, qb=qb, hcol=hcol):
                        return e.matmul(qb[:, :],
                                        sE[:, 8192 + fc * 256 + rg * 128:8192 + fc * 256 + (rg + 1) * 128],
                                        sC[:, hcol:hcol + 512],
                                        start=(fc == 0), stop=(fc == 31))
                    sch.add("pe", r_mm, reads=["wU2", f"Hs.{fc % 4}"],
                            writes=[f"pb{4 + rg}"], tag=f"Q{g}.{fc}.{rg}")

            emit_H(0)
            emit_H(1)
            for fc in range(2, 32, 2):
                emit_H(fc)
                emit_H(fc + 1)
                emit_Q(fc - 2)
                emit_Q(fc - 1)
            emit_Q(30)
            emit_Q(31)

            for rg in range(2):
                def r_ev(e, rg=rg):
                    return e.copy(sC[:, QSB[1] + rg * 512:QSB[1] + (rg + 1) * 512],
                                  pb[4 + rg][:, :])
                sch.add("act", r_ev, reads=[f"pb{4 + rg}"], writes=[f"Qs.{rg}"], tag=f"Qev{g}.{rg}")

            for ti in range(4):
                t = 4 * g + ti
                for hf in range(2):
                    fb = pb[hf]
                    for rc in range(2):
                        def r_mm(e, ti=ti, hf=hf, rc=rc, fb=fb):
                            return e.matmul(fb[:, :],
                                            sC[:, QSB[1] + rc * 512 + ti * 128:QSB[1] + rc * 512 + (ti + 1) * 128],
                                            sC[:, 5120 + rc * 1024 + hf * 512:5120 + rc * 1024 + (hf + 1) * 512],
                                            start=(rc == 0), stop=False)
                        sch.add("pe", r_mm, reads=["wV2", "Qs.0", "Qs.1"],
                                writes=[f"pb{hf}"], tag=f"ffn{t}.{hf}.{rc}")

                    def r_bias(e, hf=hf, fb=fb):
                        return e.matmul(fb[:, :],
                                        smb[32:33, ONES0:ONES0 + 128],
                                        smb[32:33, B2R + hf * 512:B2R + (hf + 1) * 512],
                                        start=False, stop=True)
                    sch.add("pe", r_bias, reads=["ones", "b2r"], writes=[f"pb{hf}"],
                            tag=f"ffnb{t}.{hf}")

                    def r_silu(e, t=t, hf=hf, fb=fb):
                        return e.activation(sB[:, t * 1024 + hf * 512:t * 1024 + (hf + 1) * 512],
                                            fb[:, :], AF.Silu)
                    gw = [f"g.{t}.{hf}"]
                    if t < 2:
                        gw += ["xnscr.0", "xnscr.1"]
                    sch.add("act", r_silu, reads=[f"pb{hf}"], writes=gw,
                            tag=f"silu{t}.{hf}")

        def emit_Vproj(t):
            for hf in range(2):
                bank = pb[2 + (2 * t + hf) % 4]
                for kc in range(8):
                    def r_mm(e, t=t, hf=hf, kc=kc, bank=bank):
                        return e.matmul(bank[:, :],
                                        xn(kc, t * 128, 128),
                                        wVQ[:, kc * 1024 + hf * 512:kc * 1024 + (hf + 1) * 512],
                                        start=(kc == 0), stop=False)
                    sch.add("pe", r_mm, reads=["wWv", f"xn.{t}"],
                            writes=[f"pb{2 + (2 * t + hf) % 4}"], tag=f"v{t}.{hf}.{kc}")

                def r_bias(e, hf=hf, bank=bank):
                    return e.matmul(bank[:, :],
                                    smb[32:33, ONES0:ONES0 + 128],
                                    smb[32:33, BVR + hf * 512:BVR + (hf + 1) * 512],
                                    start=False, stop=True)
                sch.add("pe", r_bias, reads=["ones", "bvr"], writes=[f"pb{2 + (2 * t + hf) % 4}"],
                        tag=f"vb{t}.{hf}")

                def r_ev(e, t=t, hf=hf, bank=bank):
                    return e.tensor_copy(sD[:, t * 1024 + hf * 512:t * 1024 + (hf + 1) * 512],
                                         bank[:, :])
                sch.add("dve", r_ev, reads=[f"pb{2 + (2 * t + hf) % 4}"],
                        writes=[f"vtm.{t}.{hf}"], tag=f"vev{t}.{hf}")

        emit_A_ln(0)
        for t in range(NT):
            emit_A_tr(t)
            if t + 1 < NT:
                emit_A_ln(t + 1)
            emit_Vproj(t)
        for g in range(NG):
            if PH >= 2:
                emit_C_group(g)
        if PH < 3:
            sch.finalize(nc, es)
            return nc

        dma_pool(lambda e: e.dma_start(
            out=sE[:, 0:8192].rearrange("p (kc d) -> p kc d", kc=8),
            in_=wqk_h[:, D:2 * D].rearrange("(kc p) d -> p kc d", p=128)),
            ["wWk", "wV1"], tag="ldWk")

        for g in range(NG):
            t0 = g * 512
            xn_keys = [f"xn.{4 * g + i}" for i in range(4)]
            for fg in range(8):
                bank = pb[fg % 4]
                for kc in range(8):
                    def r_mm(e, fg=fg, kc=kc, t0=t0, bank=bank):
                        return e.matmul(bank[:, :],
                                        sE[:, kc * 1024 + fg * 128:kc * 1024 + (fg + 1) * 128],
                                        xn(kc, t0, 512),
                                        start=(kc == 0), stop=(kc == 7))
                    sch.add("pe", r_mm, reads=["wWk"] + xn_keys, writes=[f"pb{fg % 4}"],
                            tag=f"k{g}.{fg}.{kc}")

                def r_ev(e, fg=fg, g=g, bank=bank):
                    return e.tensor_scalar(sC[:, fg * 2048 + g * 512:fg * 2048 + (g + 1) * 512],
                                           bank[:, :], bqk_sb[:, 8 + fg:9 + fg], None,
                                           ALU.add)
                kw = [f"kfm.{fg}.{g}"]
                if fg == 0:
                    kw += ["wU1"]
                elif fg == 1:
                    kw += ["P.0", "P.1", "Hs.0", "Hs.1"]
                elif fg == 2:
                    kw += ["Qs.0", "Qs.1", "wV2"]
                elif fg == 3:
                    kw += ["wV2", "Hs.2", "Hs.3"]
                sch.add("dve", r_ev, reads=[f"pb{fg % 4}", "bqk"], writes=kw,
                        tag=f"kev{g}.{fg}")

            def emit_trB(c):
                for kc in range(8):
                    def r_tr(e, c=c, kc=kc):
                        return e.transpose(ptA[:, kc * 128:(kc + 1) * 128],
                                           sC[:, kc * 2048 + c * 128:kc * 2048 + (c + 1) * 128],
                                           smb[:, 512:640])
                    sch.add("pe", r_tr, reads=[f"kfm.{kc}.{g}", "ident"],
                            writes=["ptA"], tag=f"trB{c}.{kc}")

                kcol = KTM0 + (c % 2) * 1024

                def r_ev(e, kcol=kcol):
                    return e.tensor_copy(sE[:, kcol:kcol + 1024], ptA[:, :])
                sch.add("dve", r_ev, reads=["ptA"], writes=[f"ktm.{c % 2}", "wU2"],
                        tag=f"ktmB{c}")

            def emit_stB(c):
                kcol = KTM0 + (c % 2) * 1024
                for h in range(H):
                    p0 = (h % 2) * 64
                    colS = (h // 2) * 64

                    def r_mm(e, c=c, h=h, p0=p0, colS=colS, kcol=kcol):
                        return e.matmul(pb[6][p0:p0 + 64, colS:colS + 64],
                                        sE[:, kcol + h * 64:kcol + h * 64 + 64],
                                        sD[:, c * 1024 + h * 64:c * 1024 + h * 64 + 64],
                                        start=(c == 0 and h <= 1),
                                        stop=(c == NT - 1 and h == H - 1),
                                        skip_group_check=True)
                    sch.add("pe", r_mm,
                            reads=[f"ktm.{c % 2}", f"vtm.{c}.{h // 8}"],
                            writes=["pb6"], tag=f"stB{c}.{h}")

            emit_trB(4 * g)
            for ci in range(1, 4):
                emit_trB(4 * g + ci)
                emit_stB(4 * g + ci - 1)
            emit_stB(4 * g + 3)

        dma_pool(lambda e: e.dma_start(
            out=wVQ[:, :].rearrange("p (kc d) -> p kc d", kc=8),
            in_=wqk_h[:, 0:D].rearrange("(kc p) d -> p kc d", p=128)),
            ["wWq", "wWv"], tag="ldWq")

        sch.add("dve", lambda e: e.tensor_copy(fscr[:, SLOC0:SLOC0 + 512], pb[6][:, :]),
                reads=["pb6"], writes=["S_loc"], tag="slocev")
        sch.add("dve", lambda e: e.tensor_scalar(fscr[:, SLOC0:SLOC0 + 512],
                                                 fscr[:, SLOC0:SLOC0 + 512],
                                                 lnscr[:, 128:129], None, ALU.mult),
                reads=["S_loc", "uv"], writes=["S_loc"], tag="slocu")
        dma_pool(lambda e: e.dma_start(out=cc_in, in_=fscr[:, SLOC0:SLOC0 + 512]),
                 ["cc_in"], reads=["S_loc"], tag="ccout")
        sch.add("pool", lambda e: e.collective_compute(
            "AllReduce", ALU.add,
            replica_groups=[[2 * i, 2 * i + 1] for i in range(N_CORES // 2)],
            ins=[cc_in.opt()], outs=[cc_out.opt()]),
            reads=["cc_in"], writes=["cc_done"], kind="cc", tag="allreduce")

        def emit_sinit():
            dma_pool(lambda e: e.dma_start(out=fscr[:, GATH0:GATH0 + 512],
                                           in_=cc_out),
                     ["gath"], reads=["cc_done", "dsq"], tag="gathin",
                     dsem="gath", dmode="serial")
            sch.add("dve", lambda e: e.tensor_scalar(srun[:, :],
                                                     fscr[:, GATH0:GATH0 + 512],
                                                     lnscr[:, 129:130], None,
                                                     ALU.mult),
                    reads=["gath", "uv"], writes=["S_run"], tag="sinit")
            sch.add("act", lambda e: e.copy(smb[:, SRBF0:SRBF0 + 512], srun[:, :]),
                    reads=["S_run"], writes=["S_runbf"], tag="srbf0")

        if PH < 4:
            sch.finalize(nc, es)
            return nc
        DE_G = int(_os.environ.get("K_DE_G", "4"))
        DE_LVL = int(_os.environ.get("K_DE_LVL", "9"))

        def emit_D_group(g):
            t0 = g * 512
            par = g % 2
            xn_keys = [f"xn.{4 * g + i}" for i in range(4)]
            for kc in range(8):
                def r_mm(e, kc=kc, t0=t0):
                    return e.matmul(pb[5][0:16, :],
                                    smb[:, WN0 + kc * 16:WN0 + (kc + 1) * 16],
                                    xn(kc, t0, 512),
                                    start=(kc == 0), stop=(kc == 7),
                                    skip_group_check=True)
                sch.add("pe", r_mm, reads=["wWn"] + xn_keys, writes=["pb5"],
                        tag=f"n{g}.{kc}")

            def r_env(e, t0=t0):
                return e.activation(smb[0:16, EN0 + t0:EN0 + t0 + 512], pb[5][0:16, :],
                                    AF.Exp, bias=bn_sb[0:16, 1:2], scale=-1.0)
            sch.add("act", r_env, reads=["pb5", "bnneg"], writes=[f"en.{g}"],
                    tag=f"env{g}")

            for fg in range(8):
                fb = pb[3 + fg % 2]

                def r_fmm(e, fg=fg, t0=t0, fb=fb):
                    return e.matmul(fb[:, :],
                                    smb[0:16, SEL0 + fg * 128:SEL0 + (fg + 1) * 128],
                                    smb[0:16, EN0 + t0:EN0 + t0 + 512],
                                    start=True, stop=True)
                sch.add("pe", r_fmm, reads=["sel", f"en.{g}"], writes=[f"pb{3 + fg % 2}"],
                        tag=f"fac{g}.{fg}")

                fcol = FACT0 if fg % 2 == 0 else 1536

                def r_fev(e, fb=fb, fcol=fcol):
                    return e.tensor_copy(fscr[:, fcol:fcol + 512], fb[:, :])
                sch.add("dve", r_fev, reads=[f"pb{3 + fg % 2}", "dsq"],
                        writes=[f"factor.{fg % 2}"], tag=f"facev{g}.{fg}")

                qb = pb[fg % 3]
                for kc in range(8):
                    def r_mm(e, fg=fg, kc=kc, t0=t0, qb=qb):
                        return e.matmul(qb[:, :],
                                        wVQ[:, kc * 1024 + fg * 128:kc * 1024 + (fg + 1) * 128],
                                        xn(kc, t0, 512),
                                        start=(kc == 0), stop=(kc == 7))
                    sch.add("pe", r_mm, reads=["wWq"] + xn_keys, writes=[f"pb{fg % 3}"],
                            tag=f"q{g}.{fg}.{kc}")

                qcol = QB0 + par * 4096 + fg * 512

                fcol2 = FACT0 if fg % 2 == 0 else 1536

                def r_qev(e, fg=fg, qcol=qcol, qb=qb, fcol2=fcol2):
                    return e.scalar_tensor_tensor(sE[:, qcol:qcol + 512], qb[:, :],
                                                  bqk_sb[:, fg:fg + 1],
                                                  fscr[:, fcol2:fcol2 + 512],
                                                  ALU.add, ALU.mult)
                sch.add("dve", r_qev, reads=[f"pb{fg % 3}", "bqk", f"factor.{fg % 2}"],
                        writes=[f"q.{par}.{fg}", "wWk"], tag=f"qev{g}.{fg}")

        def emit_E_group(g):
            par = g % 2
            for ci in range(4 if DE_LVL >= 1 else 0):
                c = 4 * g + ci
                crel = ci * 128
                for h in range(H):
                    bank = pb[h % 4]
                    p0 = (h % 2) * 64
                    fg = h // 2
                    qcol = QB0 + par * 4096 + fg * 512 + crel

                    def r_mm(e, h=h, c=c, p0=p0, fg=fg, qcol=qcol, bank=bank):
                        return e.matmul(bank[:, (h // 4) * 128:(h // 4 + 1) * 128],
                                        sC[p0:p0 + 64, fg * 2048 + c * 128:fg * 2048 + (c + 1) * 128],
                                        sE[p0:p0 + 64, qcol:qcol + 128],
                                        start=(h < 4), stop=(h >= 12),
                                        skip_group_check=True)
                    sch.add("pe", r_mm,
                            reads=[f"kfm.{fg}.{g}", f"q.{par}.{fg}"],
                            writes=[f"pb{h % 4}"], tag=f"sc{c}.{h}")

                for bk in range(4 if DE_LVL >= 2 else 0):
                    def r_ev(e, bk=bk):
                        out_ap = sE[:, SC0:SC0 + 2048] \
                            .rearrange("p (j b n) -> p j b n", b=4, n=128)[:, :, bk, :]
                        return e.tensor_tensor(out_ap,
                                               pb[bk][:, :].rearrange("p (j n) -> p j n", n=128),
                                               smb[:, 0:512].rearrange("p (j n) -> p j n", n=128),
                                               ALU.mult)
                    sch.add("dve", r_ev, reads=[f"pb{bk}", "mask"],
                            writes=[f"scs.{bk}", "wU2"], tag=f"scev{c}.{bk}")

                if DE_LVL >= 5:
                    for kc in range(8):
                        def r_tr(e, c=c, kc=kc):
                            return e.transpose(ptA[:, kc * 128:(kc + 1) * 128],
                                               sC[:, kc * 2048 + c * 128:kc * 2048 + (c + 1) * 128],
                                               smb[:, 512:640])
                        sch.add("pe", r_tr, reads=[f"kfm.{kc}.{g}", "ident"],
                                writes=["ptA"], tag=f"trE{c}.{kc}")

                    kcol = KTM0 + (c % 2) * 1024

                    def r_kev(e, kcol=kcol):
                        return e.copy(sE[:, kcol:kcol + 1024], ptA[:, :])
                    sch.add("act", r_kev, reads=["ptA"], writes=[f"ktm.{c % 2}", "wU2"],
                            tag=f"ktmE{c}")

                    for h in range(H):
                        p0 = (h % 2) * 64
                        colS = (h // 2) * 64

                        def r_mm(e, c=c, h=h, p0=p0, colS=colS, kcol=kcol):
                            return e.matmul(pb[6][p0:p0 + 64, colS:colS + 64],
                                            sE[:, kcol + h * 64:kcol + h * 64 + 64],
                                            sD[:, c * 1024 + h * 64:c * 1024 + h * 64 + 64],
                                            start=(h <= 1), stop=(h >= H - 2),
                                            skip_group_check=True)
                        sch.add("pe", r_mm,
                                reads=[f"ktm.{c % 2}", f"vtm.{c}.{h // 8}"],
                                writes=["pb6"], tag=f"stE{c}.{h}")
                for h in range(H if DE_LVL >= 3 else 0):
                    j = h // 2
                    bank = pb[4 + j // 4]
                    p0 = (h % 2) * 64
                    colc = (j % 4) * 128
                    qcol = QB0 + par * 4096 + (h // 2) * 512 + crel

                    def r_inter(e, h=h, p0=p0, colc=colc, qcol=qcol, bank=bank):
                        return e.matmul(bank[p0:p0 + 64, colc:colc + 128],
                                        smb[p0:p0 + 64, SRBF0 + (h // 2) * 64:SRBF0 + (h // 2) * 64 + 64],
                                        sE[p0:p0 + 64, qcol:qcol + 128],
                                        start=(h % 8 <= 1), stop=False,
                                        skip_group_check=True)
                    sch.add("pe", r_inter,
                            reads=["S_runbf", f"q.{par}.{h // 2}"],
                            writes=[f"pb{4 + j // 4}"], tag=f"cx{c}.{h}")
                for h in range(H if DE_LVL >= 3 else 0):
                    j = h // 2
                    bank = pb[4 + j // 4]
                    p0 = (h % 2) * 64
                    colc = (j % 4) * 128

                    def r_intra(e, h=h, c=c, p0=p0, colc=colc, bank=bank, j=j):
                        return e.matmul(bank[p0:p0 + 64, colc:colc + 128],
                                        sD[:, c * 1024 + h * 64:c * 1024 + h * 64 + 64],
                                        sE[:, SC0 + h * 128:SC0 + (h + 1) * 128],
                                        start=False, stop=(h % 8 >= 6),
                                        skip_group_check=True)
                    sch.add("pe", r_intra,
                            reads=[f"vtm.{c}.{h // 8}", f"scs.{h % 4}"],
                            writes=[f"pb{4 + j // 4}"], tag=f"ci{c}.{h}")

                if DE_LVL < 4:
                    continue
                def r_ev0(e, c=c):
                    return e.copy(
                        sA[:, :].rearrange("p (j cc) -> p j cc", j=8)
                          [:, 0:4, c * 128:(c + 1) * 128],
                        pb[4][:, :].rearrange("p (j n) -> p j n", j=4))
                sch.add("act", r_ev0, reads=["pb4"],
                        writes=[f"ctx.0.{c}", f"xn.{c}"], tag=f"cev{c}.0")

                def r_ev1(e, c=c):
                    return e.tensor_copy(
                        sA[:, :].rearrange("p (j cc) -> p j cc", j=8)
                          [:, 4:8, c * 128:(c + 1) * 128],
                        pb[5][:, :].rearrange("p (j n) -> p j n", j=4))
                sch.add("dve", r_ev1, reads=["pb5"],
                        writes=[f"ctx.1.{c}", f"xn.{c}"], tag=f"cev{c}.1")

                if DE_LVL < 5:
                    continue

                def r_supd(e):
                    return e.tensor_tensor(srun[:, :], srun[:, :], pb[6][:, :], ALU.add)
                sch.add("dve", r_supd, reads=["S_run", "pb6"], writes=["S_run"],
                        tag=f"supd{c}")

                def r_sbf(e):
                    return e.copy(smb[:, SRBF0:SRBF0 + 512], srun[:, :])
                sch.add("act", r_sbf, reads=["S_run"], writes=["S_runbf"],
                        tag=f"sbf{c}")

        _de_groups = min(NG, DE_G)
        emit_D_group(0)
        if _de_groups > 1:
            emit_D_group(1)
        emit_sinit()
        for g in range(_de_groups):
            emit_E_group(g)
            if g + 2 < _de_groups:
                emit_D_group(g + 2)
                if g + 2 == _de_groups - 1 and PH >= 5:
                    dma_pool(lambda e: e.dma_start(
                        out=wVQ[:, :].rearrange("p (jc d) -> p jc d", jc=8),
                        in_=wo_h.rearrange("(jc p) d -> p jc d", p=128)),
                        ["wWo", "wWq"], tag="ldWo")

        if PH < 5:
            sch.finalize(nc, es)
            return nc
        if not (_de_groups == NG and _de_groups >= 3):
            dma_pool(lambda e: e.dma_start(
                out=wVQ[:, :].rearrange("p (jc d) -> p jc d", jc=8),
                in_=wo_h.rearrange("(jc p) d -> p jc d", p=128)),
                ["wWo", "wWq"], tag="ldWo")

        for t in range(NT):
            buf = t % 3
            obuf = t % 2

            def r_x(e, t=t, buf=buf):
                return e.dma_start(out=xf32[:, buf * 1024:(buf + 1) * 1024],
                                   in_=x_h[t * 128:(t + 1) * 128, :])
            dma_sp(r_x, [f"xf.{buf}"], tag=f"ldxF{t}", dsem=f"xf{buf}", dmode="serial")

            for hf in range(2):
                bk = (t % 2) * 2 + hf
                bank = pb[bk]
                for jc in range(8):
                    def r_mm(e, t=t, hf=hf, jc=jc, bank=bank):
                        return e.matmul(bank[:, :],
                                        sA[:, jc * 2048 + t * 128:jc * 2048 + (t + 1) * 128],
                                        wVQ[:, jc * 1024 + hf * 512:jc * 1024 + (hf + 1) * 512],
                                        start=(jc == 0), stop=False)
                    sch.add("pe", r_mm,
                            reads=["wWo", f"ctx.0.{t}", f"ctx.1.{t}"],
                            writes=[f"pb{bk}"], tag=f"wo{t}.{hf}.{jc}")

                def r_bias(e, hf=hf, bank=bank):
                    return e.matmul(bank[:, :],
                                    smb[32:33, ONES0:ONES0 + 128],
                                    smb[32:33, BOR + hf * 512:BOR + (hf + 1) * 512],
                                    start=False, stop=True)
                sch.add("pe", r_bias, reads=["ones", "bor"], writes=[f"pb{bk}"],
                        tag=f"wob{t}.{hf}")

                ocol = obuf * 1024 + hf * 512

                def r_m1(e, t=t, hf=hf, ocol=ocol, bank=bank):
                    return e.tensor_tensor(fscr[:, ocol:ocol + 512], bank[:, :],
                                           sB[:, t * 1024 + hf * 512:t * 1024 + (hf + 1) * 512],
                                           ALU.mult)
                m1w = [f"ost.{obuf}.{hf}"]
                if t < 2:
                    m1w += ["factor.0", "factor.1", "gath", "S_loc"]
                sch.add("dve", r_m1, reads=[f"pb{bk}", f"g.{t}.{hf}"], writes=m1w,
                        tag=f"m1.{t}.{hf}")

                def r_add(e, t=t, hf=hf, ocol=ocol, buf=buf):
                    return e.tensor_tensor(fscr[:, ocol:ocol + 512],
                                           fscr[:, ocol:ocol + 512],
                                           xf32[:, buf * 1024 + hf * 512:buf * 1024 + (hf + 1) * 512],
                                           ALU.add)
                sch.add("dve", r_add, reads=[f"ost.{obuf}.{hf}", f"xf.{buf}"],
                        writes=[f"ost.{obuf}.{hf}"], tag=f"oadd.{t}.{hf}")

            def r_st(e, t=t, obuf=obuf):
                return e.dma_start(out=out_h[t * 128:(t + 1) * 128, :],
                                   in_=fscr[:, obuf * 1024:(obuf + 1) * 1024])
            dma_sp(r_st, [f"st.{t}"],
                   reads=[f"ost.{obuf}.0", f"ost.{obuf}.1"], tag=f"st{t}",
                   dsem=f"st{obuf}", dmode="serial")

        sch.finalize(nc, es)
    return nc


_NC_CACHE = {}


def _get_program():
    if "nc" not in _NC_CACHE:
        _NC_CACHE["nc"] = build_program()
    return _NC_CACHE["nc"]


def kernel(L0_attn_x, L0_attn_Wqk, L0_attn_bqk, L0_attn_Wv, L0_attn_bv,
           L0_attn_Wn, L0_attn_bn, L0_attn_Wo, L0_attn_bo,
           L0_ffn_U1, L0_ffn_V1, L0_ffn_b1, L0_ffn_U2, L0_ffn_V2, L0_ffn_b2):
    x = np.ascontiguousarray(np.asarray(L0_attn_x, dtype=np.float32))
    f32 = lambda a: np.ascontiguousarray(np.asarray(a, dtype=np.float32))
    bf16 = ml_dtypes.bfloat16

    Wqk, Wv, Wn, Wo = f32(L0_attn_Wqk), f32(L0_attn_Wv), f32(L0_attn_Wn), f32(L0_attn_Wo)
    U1, V1, U2, V2 = f32(L0_ffn_U1), f32(L0_ffn_V1), f32(L0_ffn_U2), f32(L0_ffn_V2)
    bqk, bn, bv, bo = f32(L0_attn_bqk), f32(L0_attn_bn), f32(L0_attn_bv), f32(L0_attn_bo)
    b1, b2 = f32(L0_ffn_b1), f32(L0_ffn_b2)

    bqk_t = np.ascontiguousarray(bqk.reshape(16, 128).T)
    bn_t = np.ascontiguousarray(bn.reshape(16, 1))
    b1_t = np.ascontiguousarray(b1.reshape(32, 128).T)
    bv_r = bv.reshape(1, D)
    bo_r = bo.reshape(1, D)
    b2_r = b2.reshape(1, D)

    tril = np.tril(np.ones((128, 128), np.float32)).T
    mask4 = np.ascontiguousarray(np.tile(tril, (1, 4))).astype(bf16)
    ident = np.eye(128, dtype=np.float32).astype(bf16)
    sel = np.zeros((16, 1024), np.float32)
    for fg in range(8):
        for p in range(128):
            sel[2 * fg + p // 64, fg * 128 + p] = 1.0
    sel = sel.astype(bf16)
    ones_r = np.ones((1, 128), np.float32).astype(bf16)

    nc = _get_program()
    in_maps = []
    for c in range(N_CORES):
        b, half = c // 2, c % 2
        xs = np.ascontiguousarray(x[b, half * TOK:(half + 1) * TOK, :])
        uv = np.zeros((128, 2), np.float32)
        uv[:, 0] = 1.0 - half
        uv[:, 1] = half
        in_maps.append({
            "x": xs, "Wqk": Wqk, "Wv": Wv, "Wn": Wn, "Wo": Wo,
            "U1": U1, "V1": V1, "U2": U2, "V2": V2,
            "bqk_t": bqk_t, "bn_t": bn_t, "b1_t": b1_t,
            "bv_r": bv_r, "bo_r": bo_r, "b2_r": b2_r,
            "mask4": mask4, "ident": ident, "sel": sel, "ones_r": ones_r,
            "uv": uv,
        })

    res = run_bass_kernel_spmd(nc, in_maps, core_ids=list(range(N_CORES)))
    out = np.empty((B, S, D), np.float32)
    for c in range(N_CORES):
        b, half = c // 2, c % 2
        out[b, half * TOK:(half + 1) * TOK, :] = res.results[c]["out"]
    return out


# version 19
# speedup vs baseline: 1.0314x; 1.0236x over previous
import numpy as np
import ml_dtypes
from contextlib import ExitStack

import concourse.bass as bass
import concourse.mybir as mybir
from concourse.bass_utils import run_bass_kernel_spmd

F32 = mybir.dt.float32
BF16 = mybir.dt.bfloat16
AF = mybir.ActivationFunctionType
ALU = mybir.AluOpType

B, S, D = 4, 4096, 1024
H, DH = 16, 64
FF, R = 4096, 256
TOK = 2048
NT = 16
NG = 4
EPS = 1e-5
N_CORES = 8


class _Op:
    __slots__ = ("eng", "emit", "deps", "marked", "kind", "tag", "handle", "pos",
                 "dsem")

    def __init__(self, eng, emit, kind, tag):
        self.eng, self.emit, self.kind, self.tag = eng, emit, kind, tag
        self.deps = []
        self.marked = False
        self.handle = None
        self.pos = 0
        self.dsem = None


class Sched:

    def __init__(self):
        self.streams = {e: [] for e in ("pe", "act", "dve", "pool", "sp")}
        self.last_w = {}
        self.readers = {}
        self.dgroups = {}

    def add(self, eng, emit, reads=(), writes=(), kind="c", tag="", dsem=None,
            dmode="batch"):
        op = _Op(eng, emit, kind, tag)
        if kind == "d":
            if dsem is None:
                dsem, dmode = f"_d_{tag or len(self.dgroups)}", "batch"
            op.dsem = dsem
            self.dgroups.setdefault(dsem, [dmode, []])[1].append(op)
            self.dgroups[dsem][0] = dmode
        op.pos = len(self.streams[eng])
        deps = set()
        for k in reads:
            w = self.last_w.get(k)
            if w is not None:
                deps.add(w)
        for k in writes:
            w = self.last_w.get(k)
            if w is not None:
                deps.add(w)
            for r in self.readers.get(k, ()):
                deps.add(r)
        for d in deps:
            if d is op:
                continue
            if (d.eng == eng and d.kind == "c" and kind == "c" and eng == "pe"):
                continue
            op.deps.append(d)
        self.streams[eng].append(op)
        for k in writes:
            self.last_w[k] = op
            self.readers[k] = []
        for k in reads:
            self.readers.setdefault(k, []).append(op)
        return op

    def finalize(self, nc, es):
        sems = {}
        for e in ("pe", "act", "dve", "pool"):
            sems[e] = es.enter_context(nc.semaphore(f"sem_{e}"))
        sems["cc"] = es.enter_context(nc.semaphore("sem_cc"))
        for gname in self.dgroups:
            sems[gname] = es.enter_context(nc.semaphore(f"sem_{gname}"))

        for ops in self.streams.values():
            for op in ops:
                for d in op.deps:
                    d.marked = True
        self._dfinal = {}
        for gname, (mode, gops) in self.dgroups.items():
            total = 16 * len(gops)
            for i, op in enumerate(gops):
                op.handle = (gname, 16 * (i + 1) if mode == "serial" else total)
                op.marked = True
            self._dfinal[gname] = total
        for eng, ops in self.streams.items():
            nc_, ncc = 0, 0
            for op in ops:
                if op.kind == "cc":
                    ncc += 1
                    op.handle = ("cc", ncc)
                    op.marked = True
                elif op.kind == "c" and op.marked:
                    nc_ += 1
                    op.handle = (eng, nc_)

        block = es.enter_context(nc.Block())
        engmap = {"pe": block.tensor, "act": block.scalar, "dve": block.vector,
                  "pool": block.gpsimd, "sp": block.sync}
        sched = self

        def emit_stream(eng_name):
            def body(eng):
                waited = {}
                for op in sched.streams[eng_name]:
                    for d in sorted(op.deps, key=lambda o: (o.handle[0], o.handle[1])):
                        sem_name, val = d.handle
                        if waited.get(sem_name, 0) < val:
                            eng.wait_ge(sems[sem_name], val)
                            waited[sem_name] = val
                    inst = op.emit(eng)
                    if op.marked:
                        sem_name, val = op.handle
                        inst.then_inc(sems[sem_name], 16 if op.kind == "d" else 1)
                if eng_name == "sp":
                    for gname, total in sorted(sched._dfinal.items()):
                        if waited.get(gname, 0) < total:
                            eng.wait_ge(sems[gname], total)
            return body

        for name, deco in engmap.items():
            deco(emit_stream(name))


def build_program():
    import os as _os
    PH = int(_os.environ.get("K_PHASES", "5"))
    nc = bass.Bass("TRN2", target_bir_lowering=False)

    x_h = nc.dram_tensor("x", [TOK, D], F32, kind="ExternalInput").ap()
    wqk_h = nc.dram_tensor("Wqk", [D, 2 * D], F32, kind="ExternalInput").ap()
    wv_h = nc.dram_tensor("Wv", [D, D], F32, kind="ExternalInput").ap()
    wn_h = nc.dram_tensor("Wn", [D, H], F32, kind="ExternalInput").ap()
    wo_h = nc.dram_tensor("Wo", [D, D], F32, kind="ExternalInput").ap()
    u1_h = nc.dram_tensor("U1", [D, R], F32, kind="ExternalInput").ap()
    v1_h = nc.dram_tensor("V1", [R, FF], F32, kind="ExternalInput").ap()
    u2_h = nc.dram_tensor("U2", [FF, R], F32, kind="ExternalInput").ap()
    v2_h = nc.dram_tensor("V2", [R, D], F32, kind="ExternalInput").ap()
    bqk_h = nc.dram_tensor("bqk_t", [128, 16], F32, kind="ExternalInput").ap()
    bn_h = nc.dram_tensor("bn_t", [16, 1], F32, kind="ExternalInput").ap()
    b1_h = nc.dram_tensor("b1_t", [128, 32], F32, kind="ExternalInput").ap()
    bv_h = nc.dram_tensor("bv_r", [1, D], F32, kind="ExternalInput").ap()
    bo_h = nc.dram_tensor("bo_r", [1, D], F32, kind="ExternalInput").ap()
    b2_h = nc.dram_tensor("b2_r", [1, D], F32, kind="ExternalInput").ap()
    mask_h = nc.dram_tensor("mask4", [128, 512], BF16, kind="ExternalInput").ap()
    ident_h = nc.dram_tensor("ident", [128, 128], BF16, kind="ExternalInput").ap()
    sel_h = nc.dram_tensor("sel", [16, 1024], BF16, kind="ExternalInput").ap()
    ones_h = nc.dram_tensor("ones_r", [1, 128], BF16, kind="ExternalInput").ap()
    uv_h = nc.dram_tensor("uv", [128, 2], F32, kind="ExternalInput").ap()
    out_h = nc.dram_tensor("out", [TOK, D], F32, kind="ExternalOutput").ap()

    cc_in = nc.dram_tensor("cc_in", [128, 512], F32, kind="Internal").ap()
    cc_out = nc.dram_tensor("cc_out", [128, 512], F32, kind="Internal").ap()

    es = ExitStack()
    with es:
        sA = es.enter_context(nc.sbuf_tensor("sA", [128, 16384], BF16))
        sB = es.enter_context(nc.sbuf_tensor("sB", [128, 16384], BF16))
        sC = es.enter_context(nc.sbuf_tensor("sC", [128, 16384], BF16))
        sD = es.enter_context(nc.sbuf_tensor("sD", [128, 16384], BF16))
        sE = es.enter_context(nc.sbuf_tensor("sE", [128, 16384], BF16))
        wVQ = es.enter_context(nc.sbuf_tensor("wVQ", [128, 8192], BF16))
        smb = es.enter_context(nc.sbuf_tensor("smb", [128, 4480], BF16))
        xf32 = es.enter_context(nc.sbuf_tensor("xf32", [128, 3072], F32))
        lnscr = es.enter_context(nc.sbuf_tensor("lnscr", [128, 136], F32))
        fscr = es.enter_context(nc.sbuf_tensor("fscr", [128, 2048], F32))
        srun = es.enter_context(nc.sbuf_tensor("srun", [128, 512], F32))
        bqk_sb = es.enter_context(nc.sbuf_tensor("bqk_sb", [128, 16], F32))
        b1_sb = es.enter_context(nc.sbuf_tensor("b1_sb", [128, 32], F32))
        bn_sb = es.enter_context(nc.sbuf_tensor("bn_sb", [16, 2], F32))
        pb = [es.enter_context(nc.psum_tensor(f"pb{i}", [128, 512], F32))
              for i in range(7)]
        ptA = es.enter_context(nc.psum_tensor("ptA", [128, 1024], BF16))

        def xn(kc, t0, w):
            return sA[:, kc * 2048 + t0:kc * 2048 + t0 + w]
        XNSCR = (sB, 0)
        PSB = (sC, 2048)
        HSCR = (sC, 3072)
        QSB = (sC, 4096)
        WK0 = 0
        QB0 = 0
        KTM0 = 8192
        SC0 = 10240
        MASK = smb[:, 0:512]
        IDENT = smb[:, 512:640]
        EN0 = 640
        SEL0 = 2688
        BVR, BOR, B2R = 640, 1664, 2688
        SRBF0 = 3712
        ONES0 = 4224
        WN0 = 4352
        FACT0, GATH0, SLOC0 = 0, 512, 1024
        SQD = 0

        sch = Sched()

        def dma_pool(emit, writes, reads=(), tag="", dsem=None, dmode="batch"):
            sch.add("pool", emit, reads=reads, writes=writes, kind="d", tag=tag,
                    dsem=dsem, dmode=dmode)

        def dma_sp(emit, writes, reads=(), tag="", dsem=None, dmode="batch"):
            sch.add("sp", emit, reads=reads, writes=writes, kind="d", tag=tag,
                    dsem=dsem, dmode=dmode)

        dma_sp(lambda e: e.dma_start(out=smb[:, 0:512], in_=mask_h), ["mask"], dsem="spsm")
        dma_sp(lambda e: e.dma_start(out=smb[:, 512:640], in_=ident_h), ["ident"], dsem="spsm")
        dma_sp(lambda e: e.dma_start(out=smb[0:16, SEL0:SEL0 + 1024], in_=sel_h), ["sel"], dsem="spsm")
        dma_sp(lambda e: e.dma_start(out=smb[32:33, ONES0:ONES0 + 128], in_=ones_h), ["ones"], dsem="spsm")
        dma_sp(lambda e: e.dma_start(out=bqk_sb[:, :], in_=bqk_h), ["bqk"], dsem="spsm")
        dma_sp(lambda e: e.dma_start(out=b1_sb[:, :], in_=b1_h), ["b1"], dsem="spsm")
        dma_sp(lambda e: e.dma_start(out=bn_sb[:, 0:1], in_=bn_h), ["bn"], dsem="spsm")
        dma_sp(lambda e: e.dma_start(out=lnscr[:, 128:130], in_=uv_h), ["uv"], dsem="spsm")
        dma_pool(lambda e: e.dma_start(
            out=wVQ[:, :].rearrange("p (kc d) -> p kc d", kc=8),
            in_=wv_h.rearrange("(kc p) d -> p kc d", p=128)), ["wWv"], tag="ldWv")
        dma_pool(lambda e: e.dma_start(
            out=sC[:, 0:2048].rearrange("p (kc r) -> p kc r", kc=8),
            in_=u1_h.rearrange("(kc p) r -> p kc r", p=128)), ["wU1"], tag="ldU1")
        dma_pool(lambda e: e.dma_start(
            out=sE[:, 0:8192].rearrange("p (rc f) -> p rc f", rc=2),
            in_=v1_h.rearrange("(rc p) f -> p rc f", p=128)), ["wV1"], tag="ldV1")
        dma_pool(lambda e: e.dma_start(
            out=sE[:, 8192:16384].rearrange("p (fc r) -> p fc r", fc=32),
            in_=u2_h.rearrange("(fc p) r -> p fc r", p=128)), ["wU2"], tag="ldU2")
        dma_pool(lambda e: e.dma_start(
            out=sC[:, 5120:7168].rearrange("p (rc d) -> p rc d", rc=2),
            in_=v2_h.rearrange("(rc p) d -> p rc d", p=128)), ["wV2"], tag="ldV2")
        dma_pool(lambda e: e.dma_start(out=smb[32:33, B2R:B2R + 1024], in_=b2_h), ["b2r"], dsem="pw")
        dma_pool(lambda e: e.dma_start(out=smb[32:33, BVR:BVR + 1024], in_=bv_h), ["bvr"], dsem="pw")
        dma_pool(lambda e: e.dma_start(out=smb[32:33, BOR:BOR + 1024], in_=bo_h), ["bor"], dsem="pw")
        dma_pool(lambda e: e.dma_start(
            out=smb[:, WN0:WN0 + 128].rearrange("p (kc h) -> p kc h", kc=8),
            in_=wn_h.rearrange("(kc p) h -> p kc h", p=128)),
            ["wWn"], tag="ldWn", dsem="pw")
        sch.add("dve", lambda e: e.tensor_scalar(bn_sb[0:16, 1:2], bn_sb[0:16, 0:1],
                                                 -1.0, None, ALU.mult),
                reads=["bn"], writes=["bnneg"], tag="bnneg")
        sch.add("dve", lambda e: e.memset(lnscr[:, 130:131], EPS),
                reads=(), writes=["epsc"], tag="epsset")

        def emit_A_ln(t):
            buf = t % 3

            def em(e, t=t, buf=buf):
                return e.dma_start(out=xf32[:, buf * 1024:(buf + 1) * 1024],
                                   in_=x_h[t * 128:(t + 1) * 128, :])
            dma_sp(em, [f"xf.{buf}"], tag=f"ldx{t}", dsem=f"xf{buf}", dmode="serial")

            c0 = t * 8

            def r_sum(e, t=t, buf=buf, c0=c0):
                return e.tensor_reduce(lnscr[:, c0:c0 + 1],
                                       xf32[:, buf * 1024:(buf + 1) * 1024],
                                       mybir.AxisListType.X, ALU.add, negate=True)
            sch.add("dve", r_sum, reads=[f"xf.{buf}"], writes=[f"ln.{t}"], tag=f"lnsum{t}")

            def r_sq(e, t=t, buf=buf, c0=c0):
                return e.activation(fscr[:, SQD:SQD + 1024],
                                    xf32[:, buf * 1024:(buf + 1) * 1024],
                                    AF.Square, accum_out=lnscr[:, c0 + 1:c0 + 2])
            sch.add("act", r_sq, reads=[f"xf.{buf}"], writes=["dsq", f"lnsq.{t}"], tag=f"lnsq{t}")

            def r_m(e, c0=c0):
                return e.tensor_scalar(lnscr[:, c0 + 2:c0 + 3], lnscr[:, c0:c0 + 1],
                                       1.0 / D, None, ALU.mult)
            sch.add("dve", r_m, reads=[f"ln.{t}"], writes=[f"lnm.{t}"], tag=f"lnm{t}")

            def r_msq(e, c0=c0):
                return e.tensor_tensor(lnscr[:, c0 + 3:c0 + 4], lnscr[:, c0 + 2:c0 + 3],
                                       lnscr[:, c0 + 2:c0 + 3], ALU.mult)
            sch.add("dve", r_msq, reads=[f"lnm.{t}"], writes=[f"lnmsq.{t}"], tag=f"lnmsq{t}")

            def r_var(e, c0=c0):
                return e.scalar_tensor_tensor(lnscr[:, c0 + 4:c0 + 5],
                                              lnscr[:, c0 + 1:c0 + 2], 1.0 / D,
                                              lnscr[:, c0 + 3:c0 + 4],
                                              ALU.mult, ALU.subtract)
            sch.add("dve", r_var, reads=[f"lnsq.{t}", f"lnmsq.{t}"],
                    writes=[f"lnvar.{t}"], tag=f"lnvar{t}")

            def r_std(e, c0=c0):
                return e.activation(lnscr[:, c0 + 5:c0 + 6], lnscr[:, c0 + 4:c0 + 5],
                                    AF.Sqrt, bias=lnscr[:, 130:131])
            sch.add("act", r_std, reads=[f"lnvar.{t}", "epsc"], writes=[f"lnstd.{t}"],
                    tag=f"lnstd{t}")

            def r_rstd(e, c0=c0):
                return e.reciprocal(lnscr[:, c0 + 6:c0 + 7], lnscr[:, c0 + 5:c0 + 6])
            sch.add("dve", r_rstd, reads=[f"lnstd.{t}"], writes=[f"lnrstd.{t}"], tag=f"lnrstd{t}")

            sbuf_t, scr0 = XNSCR
            sbuf2 = t % 2
            xcol = scr0 + sbuf2 * 1024

            def r_xn(e, t=t, buf=buf, c0=c0, xcol=xcol):
                return e.tensor_scalar(sB[:, xcol:xcol + 1024],
                                       xf32[:, buf * 1024:(buf + 1) * 1024],
                                       lnscr[:, c0 + 2:c0 + 3],
                                       lnscr[:, c0 + 6:c0 + 7],
                                       ALU.add, ALU.mult)
            sch.add("dve", r_xn, reads=[f"xf.{buf}", f"lnm.{t}", f"lnrstd.{t}"],
                    writes=[f"xnscr.{sbuf2}"], tag=f"xn{t}")

        def emit_A_tr(t):
            sbuf2 = t % 2
            xcol = XNSCR[1] + sbuf2 * 1024
            for kc in range(8):
                def r_tr(e, kc=kc, xcol=xcol):
                    return e.transpose(ptA[:, kc * 128:(kc + 1) * 128],
                                       sB[:, xcol + kc * 128:xcol + (kc + 1) * 128],
                                       smb[:, 512:640])
                sch.add("pe", r_tr, reads=[f"xnscr.{sbuf2}", "ident"],
                        writes=["ptA"], tag=f"trA{t}.{kc}")

            def r_ev(e, t=t):
                return e.tensor_copy(
                    sA[:, :].rearrange("p (kc c) -> p kc c", kc=8)[:, :, t * 128:(t + 1) * 128],
                    ptA[:, :].rearrange("p (kc n) -> p kc n", kc=8))
            sch.add("dve", r_ev, reads=["ptA"], writes=[f"xn.{t}"], tag=f"evA{t}")

        def emit_A_tile(t):
            emit_A_ln(t)
            emit_A_tr(t)

        def emit_C_group(g, a_tiles=()):
            t0 = g * 512
            xn_keys = [f"xn.{4 * g + i}" for i in range(4)]
            for rg in range(2):
                bank = pb[rg]
                for kc in range(8):
                    def r_mm(e, rg=rg, kc=kc, t0=t0, bank=bank):
                        return e.matmul(bank[:, :],
                                        sC[:, kc * 256 + rg * 128:kc * 256 + (rg + 1) * 128],
                                        xn(kc, t0, 512),
                                        start=(kc == 0), stop=(kc == 7))
                    sch.add("pe", r_mm, reads=["wU1"] + xn_keys, writes=[f"pb{rg}"],
                            tag=f"P{g}.{rg}.{kc}")

                def r_ev(e, rg=rg, bank=bank):
                    return e.copy(sC[:, PSB[1] + rg * 512:PSB[1] + (rg + 1) * 512], bank[:, :])
                sch.add("act", r_ev, reads=[f"pb{rg}"], writes=[f"P.{rg}"], tag=f"Pev{g}.{rg}")

            def emit_H(fc):
                hb = pb[fc % 4]
                for rc in range(2):
                    def r_mm(e, fc=fc, rc=rc, hb=hb):
                        return e.matmul(hb[:, :],
                                        sE[:, rc * 4096 + fc * 128:rc * 4096 + (fc + 1) * 128],
                                        sC[:, PSB[1] + rc * 512:PSB[1] + (rc + 1) * 512],
                                        start=(rc == 0), stop=(rc == 1))
                    sch.add("pe", r_mm, reads=["wV1", "P.0", "P.1"],
                            writes=[f"pb{fc % 4}"], tag=f"H{g}.{fc}.{rc}")

                hcol = (3072, 3584, 7168, 7680)[fc % 4]

                def r_gelu(e, fc=fc, hb=hb, hcol=hcol):
                    return e.activation(sC[:, hcol:hcol + 512], hb[:, :],
                                        AF.Gelu_apprx_tanh, bias=b1_sb[:, fc:fc + 1])
                sch.add("act", r_gelu, reads=[f"pb{fc % 4}", "b1"],
                        writes=[f"Hs.{fc % 4}"], tag=f"gelu{g}.{fc}")

            def emit_Q(fc):
                hcol = (3072, 3584, 7168, 7680)[fc % 4]
                for rg in range(2):
                    qb = pb[4 + rg]

                    def r_mm(e, fc=fc, rg=rg, qb=qb, hcol=hcol):
                        return e.matmul(qb[:, :],
                                        sE[:, 8192 + fc * 256 + rg * 128:8192 + fc * 256 + (rg + 1) * 128],
                                        sC[:, hcol:hcol + 512],
                                        start=(fc == 0), stop=(fc == 31))
                    sch.add("pe", r_mm, reads=["wU2", f"Hs.{fc % 4}"],
                            writes=[f"pb{4 + rg}"], tag=f"Q{g}.{fc}.{rg}")

            emit_H(0)
            emit_H(1)
            for fc in range(2, 32, 2):
                emit_H(fc)
                emit_H(fc + 1)
                emit_Q(fc - 2)
                emit_Q(fc - 1)
            emit_Q(30)
            emit_Q(31)

            for rg in range(2):
                def r_ev(e, rg=rg):
                    return e.copy(sC[:, QSB[1] + rg * 512:QSB[1] + (rg + 1) * 512],
                                  pb[4 + rg][:, :])
                sch.add("act", r_ev, reads=[f"pb{4 + rg}"], writes=[f"Qs.{rg}"], tag=f"Qev{g}.{rg}")

            for ti in range(4):
                t = 4 * g + ti
                for hf in range(2):
                    fb = pb[hf]
                    for rc in range(2):
                        def r_mm(e, ti=ti, hf=hf, rc=rc, fb=fb):
                            return e.matmul(fb[:, :],
                                            sC[:, QSB[1] + rc * 512 + ti * 128:QSB[1] + rc * 512 + (ti + 1) * 128],
                                            sC[:, 5120 + rc * 1024 + hf * 512:5120 + rc * 1024 + (hf + 1) * 512],
                                            start=(rc == 0), stop=False)
                        sch.add("pe", r_mm, reads=["wV2", "Qs.0", "Qs.1"],
                                writes=[f"pb{hf}"], tag=f"ffn{t}.{hf}.{rc}")

                    def r_bias(e, hf=hf, fb=fb):
                        return e.matmul(fb[:, :],
                                        smb[32:33, ONES0:ONES0 + 128],
                                        smb[32:33, B2R + hf * 512:B2R + (hf + 1) * 512],
                                        start=False, stop=True)
                    sch.add("pe", r_bias, reads=["ones", "b2r"], writes=[f"pb{hf}"],
                            tag=f"ffnb{t}.{hf}")

                    def r_silu(e, t=t, hf=hf, fb=fb):
                        return e.activation(sB[:, t * 1024 + hf * 512:t * 1024 + (hf + 1) * 512],
                                            fb[:, :], AF.Silu)
                    gw = [f"g.{t}.{hf}"]
                    if t < 2:
                        gw += ["xnscr.0", "xnscr.1"]
                    sch.add("act", r_silu, reads=[f"pb{hf}"], writes=gw,
                            tag=f"silu{t}.{hf}")

        def emit_Vproj(t):
            for hf in range(2):
                bank = pb[2 + (2 * t + hf) % 4]
                for kc in range(8):
                    def r_mm(e, t=t, hf=hf, kc=kc, bank=bank):
                        return e.matmul(bank[:, :],
                                        xn(kc, t * 128, 128),
                                        wVQ[:, kc * 1024 + hf * 512:kc * 1024 + (hf + 1) * 512],
                                        start=(kc == 0), stop=False)
                    sch.add("pe", r_mm, reads=["wWv", f"xn.{t}"],
                            writes=[f"pb{2 + (2 * t + hf) % 4}"], tag=f"v{t}.{hf}.{kc}")

                def r_bias(e, hf=hf, bank=bank):
                    return e.matmul(bank[:, :],
                                    smb[32:33, ONES0:ONES0 + 128],
                                    smb[32:33, BVR + hf * 512:BVR + (hf + 1) * 512],
                                    start=False, stop=True)
                sch.add("pe", r_bias, reads=["ones", "bvr"], writes=[f"pb{2 + (2 * t + hf) % 4}"],
                        tag=f"vb{t}.{hf}")

                def r_ev(e, t=t, hf=hf, bank=bank):
                    return e.tensor_copy(sD[:, t * 1024 + hf * 512:t * 1024 + (hf + 1) * 512],
                                         bank[:, :])
                sch.add("dve", r_ev, reads=[f"pb{2 + (2 * t + hf) % 4}"],
                        writes=[f"vtm.{t}.{hf}"], tag=f"vev{t}.{hf}")

        emit_A_ln(0)
        for t in range(NT):
            emit_A_tr(t)
            if t + 1 < NT:
                emit_A_ln(t + 1)
            emit_Vproj(t)
        for g in range(NG):
            if PH >= 2:
                emit_C_group(g)
        if PH < 3:
            sch.finalize(nc, es)
            return nc

        dma_pool(lambda e: e.dma_start(
            out=sE[:, 0:8192].rearrange("p (kc d) -> p kc d", kc=8),
            in_=wqk_h[:, D:2 * D].rearrange("(kc p) d -> p kc d", p=128)),
            ["wWk", "wV1"], tag="ldWk")

        for g in range(NG):
            t0 = g * 512
            xn_keys = [f"xn.{4 * g + i}" for i in range(4)]
            for fg in range(8):
                bank = pb[fg % 4]
                for kc in range(8):
                    def r_mm(e, fg=fg, kc=kc, t0=t0, bank=bank):
                        return e.matmul(bank[:, :],
                                        sE[:, kc * 1024 + fg * 128:kc * 1024 + (fg + 1) * 128],
                                        xn(kc, t0, 512),
                                        start=(kc == 0), stop=(kc == 7))
                    sch.add("pe", r_mm, reads=["wWk"] + xn_keys, writes=[f"pb{fg % 4}"],
                            tag=f"k{g}.{fg}.{kc}")

                def r_ev(e, fg=fg, g=g, bank=bank):
                    return e.tensor_scalar(sC[:, fg * 2048 + g * 512:fg * 2048 + (g + 1) * 512],
                                           bank[:, :], bqk_sb[:, 8 + fg:9 + fg], None,
                                           ALU.add)
                kw = [f"kfm.{fg}.{g}"]
                if fg == 0:
                    kw += ["wU1"]
                elif fg == 1:
                    kw += ["P.0", "P.1", "Hs.0", "Hs.1"]
                elif fg == 2:
                    kw += ["Qs.0", "Qs.1", "wV2"]
                elif fg == 3:
                    kw += ["wV2", "Hs.2", "Hs.3"]
                sch.add("dve", r_ev, reads=[f"pb{fg % 4}", "bqk"], writes=kw,
                        tag=f"kev{g}.{fg}")

            def emit_trB(c):
                for kc in range(8):
                    def r_tr(e, c=c, kc=kc):
                        return e.transpose(ptA[:, kc * 128:(kc + 1) * 128],
                                           sC[:, kc * 2048 + c * 128:kc * 2048 + (c + 1) * 128],
                                           smb[:, 512:640])
                    sch.add("pe", r_tr, reads=[f"kfm.{kc}.{g}", "ident"],
                            writes=["ptA"], tag=f"trB{c}.{kc}")

                kcol = KTM0 + (c % 2) * 1024

                def r_ev(e, kcol=kcol):
                    return e.tensor_copy(sE[:, kcol:kcol + 1024], ptA[:, :])
                sch.add("dve", r_ev, reads=["ptA"], writes=[f"ktm.{c % 2}", "wU2"],
                        tag=f"ktmB{c}")

            def emit_stB(c):
                kcol = KTM0 + (c % 2) * 1024
                for h in range(H):
                    p0 = (h % 2) * 64
                    colS = (h // 2) * 64

                    def r_mm(e, c=c, h=h, p0=p0, colS=colS, kcol=kcol):
                        return e.matmul(pb[6][p0:p0 + 64, colS:colS + 64],
                                        sE[:, kcol + h * 64:kcol + h * 64 + 64],
                                        sD[:, c * 1024 + h * 64:c * 1024 + h * 64 + 64],
                                        start=(c == 0 and h <= 1),
                                        stop=(c == NT - 1 and h == H - 1),
                                        skip_group_check=True)
                    sch.add("pe", r_mm,
                            reads=[f"ktm.{c % 2}", f"vtm.{c}.{h // 8}"],
                            writes=["pb6"], tag=f"stB{c}.{h}")

            emit_trB(4 * g)
            for ci in range(1, 4):
                emit_trB(4 * g + ci)
                emit_stB(4 * g + ci - 1)
            emit_stB(4 * g + 3)

        dma_pool(lambda e: e.dma_start(
            out=wVQ[:, :].rearrange("p (kc d) -> p kc d", kc=8),
            in_=wqk_h[:, 0:D].rearrange("(kc p) d -> p kc d", p=128)),
            ["wWq", "wWv"], tag="ldWq")

        sch.add("dve", lambda e: e.tensor_copy(fscr[:, SLOC0:SLOC0 + 512], pb[6][:, :]),
                reads=["pb6"], writes=["S_loc"], tag="slocev")
        sch.add("dve", lambda e: e.tensor_scalar(fscr[:, SLOC0:SLOC0 + 512],
                                                 fscr[:, SLOC0:SLOC0 + 512],
                                                 lnscr[:, 128:129], None, ALU.mult),
                reads=["S_loc", "uv"], writes=["S_loc"], tag="slocu")
        dma_pool(lambda e: e.dma_start(out=cc_in, in_=fscr[:, SLOC0:SLOC0 + 512]),
                 ["cc_in"], reads=["S_loc"], tag="ccout")
        sch.add("pool", lambda e: e.collective_compute(
            "AllReduce", ALU.add,
            replica_groups=[[2 * i, 2 * i + 1] for i in range(N_CORES // 2)],
            ins=[cc_in.opt()], outs=[cc_out.opt()]),
            reads=["cc_in"], writes=["cc_done"], kind="cc", tag="allreduce")

        def emit_sinit():
            dma_pool(lambda e: e.dma_start(out=fscr[:, GATH0:GATH0 + 512],
                                           in_=cc_out),
                     ["gath"], reads=["cc_done", "dsq"], tag="gathin",
                     dsem="gath", dmode="serial")
            sch.add("dve", lambda e: e.tensor_scalar(srun[:, :],
                                                     fscr[:, GATH0:GATH0 + 512],
                                                     lnscr[:, 129:130], None,
                                                     ALU.mult),
                    reads=["gath", "uv"], writes=["S_run"], tag="sinit")
            sch.add("act", lambda e: e.copy(smb[:, SRBF0:SRBF0 + 512], srun[:, :]),
                    reads=["S_run"], writes=["S_runbf"], tag="srbf0")

        if PH < 4:
            sch.finalize(nc, es)
            return nc
        DE_G = int(_os.environ.get("K_DE_G", "4"))
        DE_LVL = int(_os.environ.get("K_DE_LVL", "9"))

        def emit_D_group(g):
            t0 = g * 512
            par = g % 2
            xn_keys = [f"xn.{4 * g + i}" for i in range(4)]
            for kc in range(8):
                def r_mm(e, kc=kc, t0=t0):
                    return e.matmul(pb[5][0:16, :],
                                    smb[:, WN0 + kc * 16:WN0 + (kc + 1) * 16],
                                    xn(kc, t0, 512),
                                    start=(kc == 0), stop=(kc == 7),
                                    skip_group_check=True)
                sch.add("pe", r_mm, reads=["wWn"] + xn_keys, writes=["pb5"],
                        tag=f"n{g}.{kc}")

            def r_env(e, t0=t0):
                return e.activation(smb[0:16, EN0 + t0:EN0 + t0 + 512], pb[5][0:16, :],
                                    AF.Exp, bias=bn_sb[0:16, 1:2], scale=-1.0)
            sch.add("act", r_env, reads=["pb5", "bnneg"], writes=[f"en.{g}"],
                    tag=f"env{g}")

            for fg in range(8):
                fb = pb[3 + fg % 2]

                def r_fmm(e, fg=fg, t0=t0, fb=fb):
                    return e.matmul(fb[:, :],
                                    smb[0:16, SEL0 + fg * 128:SEL0 + (fg + 1) * 128],
                                    smb[0:16, EN0 + t0:EN0 + t0 + 512],
                                    start=True, stop=True)
                sch.add("pe", r_fmm, reads=["sel", f"en.{g}"], writes=[f"pb{3 + fg % 2}"],
                        tag=f"fac{g}.{fg}")

                fcol = FACT0 if fg % 2 == 0 else 1536

                def r_fev(e, fb=fb, fcol=fcol):
                    return e.copy(fscr[:, fcol:fcol + 512], fb[:, :])
                sch.add("act", r_fev, reads=[f"pb{3 + fg % 2}", "dsq"],
                        writes=[f"factor.{fg % 2}"], tag=f"facev{g}.{fg}")

                qb = pb[fg % 3]
                for kc in range(8):
                    def r_mm(e, fg=fg, kc=kc, t0=t0, qb=qb):
                        return e.matmul(qb[:, :],
                                        wVQ[:, kc * 1024 + fg * 128:kc * 1024 + (fg + 1) * 128],
                                        xn(kc, t0, 512),
                                        start=(kc == 0), stop=(kc == 7))
                    sch.add("pe", r_mm, reads=["wWq"] + xn_keys, writes=[f"pb{fg % 3}"],
                            tag=f"q{g}.{fg}.{kc}")

                qcol = QB0 + par * 4096 + fg * 512

                fcol2 = FACT0 if fg % 2 == 0 else 1536

                def r_qev(e, fg=fg, qcol=qcol, qb=qb, fcol2=fcol2):
                    return e.scalar_tensor_tensor(sE[:, qcol:qcol + 512], qb[:, :],
                                                  bqk_sb[:, fg:fg + 1],
                                                  fscr[:, fcol2:fcol2 + 512],
                                                  ALU.add, ALU.mult)
                sch.add("dve", r_qev, reads=[f"pb{fg % 3}", "bqk", f"factor.{fg % 2}"],
                        writes=[f"q.{par}.{fg}", "wWk"], tag=f"qev{g}.{fg}")

        def emit_E_group(g):
            par = g % 2
            for ci in range(4 if DE_LVL >= 1 else 0):
                c = 4 * g + ci
                crel = ci * 128
                for h in range(H):
                    bank = pb[h % 4]
                    p0 = (h % 2) * 64
                    fg = h // 2
                    qcol = QB0 + par * 4096 + fg * 512 + crel

                    def r_mm(e, h=h, c=c, p0=p0, fg=fg, qcol=qcol, bank=bank):
                        return e.matmul(bank[:, (h // 4) * 128:(h // 4 + 1) * 128],
                                        sC[p0:p0 + 64, fg * 2048 + c * 128:fg * 2048 + (c + 1) * 128],
                                        sE[p0:p0 + 64, qcol:qcol + 128],
                                        start=(h < 4), stop=(h >= 12),
                                        skip_group_check=True)
                    sch.add("pe", r_mm,
                            reads=[f"kfm.{fg}.{g}", f"q.{par}.{fg}"],
                            writes=[f"pb{h % 4}"], tag=f"sc{c}.{h}")

                for bk in range(4 if DE_LVL >= 2 else 0):
                    def r_ev(e, bk=bk):
                        out_ap = sE[:, SC0:SC0 + 2048] \
                            .rearrange("p (j b n) -> p j b n", b=4, n=128)[:, :, bk, :]
                        return e.tensor_tensor(out_ap,
                                               pb[bk][:, :].rearrange("p (j n) -> p j n", n=128),
                                               smb[:, 0:512].rearrange("p (j n) -> p j n", n=128),
                                               ALU.mult)
                    sch.add("dve", r_ev, reads=[f"pb{bk}", "mask"],
                            writes=[f"scs.{bk}", "wU2"], tag=f"scev{c}.{bk}")

                if DE_LVL >= 5:
                    for kc in range(8):
                        def r_tr(e, c=c, kc=kc):
                            return e.transpose(ptA[:, kc * 128:(kc + 1) * 128],
                                               sC[:, kc * 2048 + c * 128:kc * 2048 + (c + 1) * 128],
                                               smb[:, 512:640])
                        sch.add("pe", r_tr, reads=[f"kfm.{kc}.{g}", "ident"],
                                writes=["ptA"], tag=f"trE{c}.{kc}")

                    kcol = KTM0 + (c % 2) * 1024

                    def r_kev(e, kcol=kcol):
                        return e.copy(sE[:, kcol:kcol + 1024], ptA[:, :])
                    sch.add("act", r_kev, reads=["ptA"], writes=[f"ktm.{c % 2}", "wU2"],
                            tag=f"ktmE{c}")

                    for h in range(H):
                        p0 = (h % 2) * 64
                        colS = (h // 2) * 64

                        def r_mm(e, c=c, h=h, p0=p0, colS=colS, kcol=kcol):
                            return e.matmul(pb[6][p0:p0 + 64, colS:colS + 64],
                                            sE[:, kcol + h * 64:kcol + h * 64 + 64],
                                            sD[:, c * 1024 + h * 64:c * 1024 + h * 64 + 64],
                                            start=(h <= 1), stop=(h >= H - 2),
                                            skip_group_check=True)
                        sch.add("pe", r_mm,
                                reads=[f"ktm.{c % 2}", f"vtm.{c}.{h // 8}"],
                                writes=["pb6"], tag=f"stE{c}.{h}")
                for h in range(H if DE_LVL >= 3 else 0):
                    j = h // 2
                    bank = pb[4 + j // 4]
                    p0 = (h % 2) * 64
                    colc = (j % 4) * 128
                    qcol = QB0 + par * 4096 + (h // 2) * 512 + crel

                    def r_inter(e, h=h, p0=p0, colc=colc, qcol=qcol, bank=bank):
                        return e.matmul(bank[p0:p0 + 64, colc:colc + 128],
                                        smb[p0:p0 + 64, SRBF0 + (h // 2) * 64:SRBF0 + (h // 2) * 64 + 64],
                                        sE[p0:p0 + 64, qcol:qcol + 128],
                                        start=(h % 8 <= 1), stop=False,
                                        skip_group_check=True)
                    sch.add("pe", r_inter,
                            reads=["S_runbf", f"q.{par}.{h // 2}"],
                            writes=[f"pb{4 + j // 4}"], tag=f"cx{c}.{h}")
                for h in range(H if DE_LVL >= 3 else 0):
                    j = h // 2
                    bank = pb[4 + j // 4]
                    p0 = (h % 2) * 64
                    colc = (j % 4) * 128

                    def r_intra(e, h=h, c=c, p0=p0, colc=colc, bank=bank, j=j):
                        return e.matmul(bank[p0:p0 + 64, colc:colc + 128],
                                        sD[:, c * 1024 + h * 64:c * 1024 + h * 64 + 64],
                                        sE[:, SC0 + h * 128:SC0 + (h + 1) * 128],
                                        start=False, stop=(h % 8 >= 6),
                                        skip_group_check=True)
                    sch.add("pe", r_intra,
                            reads=[f"vtm.{c}.{h // 8}", f"scs.{h % 4}"],
                            writes=[f"pb{4 + j // 4}"], tag=f"ci{c}.{h}")

                if DE_LVL < 4:
                    continue
                def r_ev0(e, c=c):
                    return e.copy(
                        sA[:, :].rearrange("p (j cc) -> p j cc", j=8)
                          [:, 0:4, c * 128:(c + 1) * 128],
                        pb[4][:, :].rearrange("p (j n) -> p j n", j=4))
                sch.add("act", r_ev0, reads=["pb4"],
                        writes=[f"ctx.0.{c}", f"xn.{c}"], tag=f"cev{c}.0")

                def r_ev1(e, c=c):
                    return e.copy(
                        sA[:, :].rearrange("p (j cc) -> p j cc", j=8)
                          [:, 4:8, c * 128:(c + 1) * 128],
                        pb[5][:, :].rearrange("p (j n) -> p j n", j=4))
                sch.add("act", r_ev1, reads=["pb5"],
                        writes=[f"ctx.1.{c}", f"xn.{c}"], tag=f"cev{c}.1")

                if DE_LVL < 5:
                    continue

                def r_supd(e):
                    return e.tensor_tensor(srun[:, :], srun[:, :], pb[6][:, :], ALU.add)
                sch.add("dve", r_supd, reads=["S_run", "pb6"], writes=["S_run"],
                        tag=f"supd{c}")

                def r_sbf(e):
                    return e.copy(smb[:, SRBF0:SRBF0 + 512], srun[:, :])
                sch.add("act", r_sbf, reads=["S_run"], writes=["S_runbf"],
                        tag=f"sbf{c}")

        _de_groups = min(NG, DE_G)
        emit_D_group(0)
        if _de_groups > 1:
            emit_D_group(1)
        emit_sinit()
        for g in range(_de_groups):
            emit_E_group(g)
            if g + 2 < _de_groups:
                emit_D_group(g + 2)
                if g + 2 == _de_groups - 1 and PH >= 5:
                    dma_pool(lambda e: e.dma_start(
                        out=wVQ[:, :].rearrange("p (jc d) -> p jc d", jc=8),
                        in_=wo_h.rearrange("(jc p) d -> p jc d", p=128)),
                        ["wWo", "wWq"], tag="ldWo")

        if PH < 5:
            sch.finalize(nc, es)
            return nc
        if not (_de_groups == NG and _de_groups >= 3):
            dma_pool(lambda e: e.dma_start(
                out=wVQ[:, :].rearrange("p (jc d) -> p jc d", jc=8),
                in_=wo_h.rearrange("(jc p) d -> p jc d", p=128)),
                ["wWo", "wWq"], tag="ldWo")

        for t in range(NT):
            buf = t % 3
            obuf = t % 2

            def r_x(e, t=t, buf=buf):
                return e.dma_start(out=xf32[:, buf * 1024:(buf + 1) * 1024],
                                   in_=x_h[t * 128:(t + 1) * 128, :])
            dma_sp(r_x, [f"xf.{buf}"], tag=f"ldxF{t}", dsem=f"xf{buf}", dmode="serial")

            for hf in range(2):
                bk = (t % 2) * 2 + hf
                bank = pb[bk]
                for jc in range(8):
                    def r_mm(e, t=t, hf=hf, jc=jc, bank=bank):
                        return e.matmul(bank[:, :],
                                        sA[:, jc * 2048 + t * 128:jc * 2048 + (t + 1) * 128],
                                        wVQ[:, jc * 1024 + hf * 512:jc * 1024 + (hf + 1) * 512],
                                        start=(jc == 0), stop=False)
                    sch.add("pe", r_mm,
                            reads=["wWo", f"ctx.0.{t}", f"ctx.1.{t}"],
                            writes=[f"pb{bk}"], tag=f"wo{t}.{hf}.{jc}")

                def r_bias(e, hf=hf, bank=bank):
                    return e.matmul(bank[:, :],
                                    smb[32:33, ONES0:ONES0 + 128],
                                    smb[32:33, BOR + hf * 512:BOR + (hf + 1) * 512],
                                    start=False, stop=True)
                sch.add("pe", r_bias, reads=["ones", "bor"], writes=[f"pb{bk}"],
                        tag=f"wob{t}.{hf}")

                ocol = obuf * 1024 + hf * 512

                def r_m1(e, t=t, hf=hf, ocol=ocol, bank=bank):
                    return e.tensor_tensor(fscr[:, ocol:ocol + 512], bank[:, :],
                                           sB[:, t * 1024 + hf * 512:t * 1024 + (hf + 1) * 512],
                                           ALU.mult)
                m1w = [f"ost.{obuf}.{hf}"]
                if t < 2:
                    m1w += ["factor.0", "factor.1", "gath", "S_loc"]
                sch.add("dve", r_m1, reads=[f"pb{bk}", f"g.{t}.{hf}"], writes=m1w,
                        tag=f"m1.{t}.{hf}")

                def r_add(e, t=t, hf=hf, ocol=ocol, buf=buf):
                    return e.tensor_tensor(fscr[:, ocol:ocol + 512],
                                           fscr[:, ocol:ocol + 512],
                                           xf32[:, buf * 1024 + hf * 512:buf * 1024 + (hf + 1) * 512],
                                           ALU.add)
                sch.add("dve", r_add, reads=[f"ost.{obuf}.{hf}", f"xf.{buf}"],
                        writes=[f"ost.{obuf}.{hf}"], tag=f"oadd.{t}.{hf}")

            def r_st(e, t=t, obuf=obuf):
                return e.dma_start(out=out_h[t * 128:(t + 1) * 128, :],
                                   in_=fscr[:, obuf * 1024:(obuf + 1) * 1024])
            dma_sp(r_st, [f"st.{t}"],
                   reads=[f"ost.{obuf}.0", f"ost.{obuf}.1"], tag=f"st{t}",
                   dsem=f"st{obuf}", dmode="serial")

        sch.finalize(nc, es)
    return nc


_NC_CACHE = {}


def _get_program():
    if "nc" not in _NC_CACHE:
        _NC_CACHE["nc"] = build_program()
    return _NC_CACHE["nc"]


def kernel(L0_attn_x, L0_attn_Wqk, L0_attn_bqk, L0_attn_Wv, L0_attn_bv,
           L0_attn_Wn, L0_attn_bn, L0_attn_Wo, L0_attn_bo,
           L0_ffn_U1, L0_ffn_V1, L0_ffn_b1, L0_ffn_U2, L0_ffn_V2, L0_ffn_b2):
    x = np.ascontiguousarray(np.asarray(L0_attn_x, dtype=np.float32))
    f32 = lambda a: np.ascontiguousarray(np.asarray(a, dtype=np.float32))
    bf16 = ml_dtypes.bfloat16

    Wqk, Wv, Wn, Wo = f32(L0_attn_Wqk), f32(L0_attn_Wv), f32(L0_attn_Wn), f32(L0_attn_Wo)
    U1, V1, U2, V2 = f32(L0_ffn_U1), f32(L0_ffn_V1), f32(L0_ffn_U2), f32(L0_ffn_V2)
    bqk, bn, bv, bo = f32(L0_attn_bqk), f32(L0_attn_bn), f32(L0_attn_bv), f32(L0_attn_bo)
    b1, b2 = f32(L0_ffn_b1), f32(L0_ffn_b2)

    bqk_t = np.ascontiguousarray(bqk.reshape(16, 128).T)
    bn_t = np.ascontiguousarray(bn.reshape(16, 1))
    b1_t = np.ascontiguousarray(b1.reshape(32, 128).T)
    bv_r = bv.reshape(1, D)
    bo_r = bo.reshape(1, D)
    b2_r = b2.reshape(1, D)

    tril = np.tril(np.ones((128, 128), np.float32)).T
    mask4 = np.ascontiguousarray(np.tile(tril, (1, 4))).astype(bf16)
    ident = np.eye(128, dtype=np.float32).astype(bf16)
    sel = np.zeros((16, 1024), np.float32)
    for fg in range(8):
        for p in range(128):
            sel[2 * fg + p // 64, fg * 128 + p] = 1.0
    sel = sel.astype(bf16)
    ones_r = np.ones((1, 128), np.float32).astype(bf16)

    nc = _get_program()
    in_maps = []
    for c in range(N_CORES):
        b, half = c // 2, c % 2
        xs = np.ascontiguousarray(x[b, half * TOK:(half + 1) * TOK, :])
        uv = np.zeros((128, 2), np.float32)
        uv[:, 0] = 1.0 - half
        uv[:, 1] = half
        in_maps.append({
            "x": xs, "Wqk": Wqk, "Wv": Wv, "Wn": Wn, "Wo": Wo,
            "U1": U1, "V1": V1, "U2": U2, "V2": V2,
            "bqk_t": bqk_t, "bn_t": bn_t, "b1_t": b1_t,
            "bv_r": bv_r, "bo_r": bo_r, "b2_r": b2_r,
            "mask4": mask4, "ident": ident, "sel": sel, "ones_r": ones_r,
            "uv": uv,
        })

    res = run_bass_kernel_spmd(nc, in_maps, core_ids=list(range(N_CORES)))
    out = np.empty((B, S, D), np.float32)
    for c in range(N_CORES):
        b, half = c // 2, c % 2
        out[b, half * TOK:(half + 1) * TOK, :] = res.results[c]["out"]
    return out
